# version 08
# baseline (speedup 1.0000x reference)
import numpy as np
import ml_dtypes

try:
    import concourse.bass as bass
except ImportError:
    import sys
    sys.path.insert(0, "/opt/trn_rl_repo")
    import concourse.bass as bass
import concourse.mybir as mybir
from concourse.bass_utils import run_bass_kernel_spmd

F32 = mybir.dt.float32
BF16 = mybir.dt.bfloat16
AF = mybir.ActivationFunctionType
ALU = mybir.AluOpType
AX = mybir.AxisListType
f32 = np.float32

NCORES = 8
SEQ = 2048
NSEQ = 2
DM = 1024
TPC = SEQ * NSEQ
B_L0 = 12.0
B_L1 = 3.0

SPLIT_LAYERS = False
PHASES = {"attn", "wout", "ffn"}


def _pos_enc():
    pos = np.arange(SEQ, dtype=np.float64)[:, None]
    div = np.exp(np.arange(0, 64, 2, dtype=np.float64) * -(np.log(10000.0) / 64))
    pe = np.zeros((SEQ, 64))
    pe[:, 0::2] = np.sin(pos * div)
    pe[:, 1::2] = np.cos(pos * div)
    pe_t = pe.T.astype(np.float32)
    return np.ascontiguousarray(np.concatenate([pe_t, pe_t], axis=0))


def _rope_cs():
    t = np.arange(1, SEQ + 1, dtype=np.float64)[:, None]
    i = np.arange(32, dtype=np.float64)
    theta = (1.0 / 10000.0) ** (2.0 * i / 64.0)
    ang = t * theta
    cos = np.cos(ang).T.astype(np.float32)
    sin = np.sin(ang).T.astype(np.float32)
    return np.ascontiguousarray(np.concatenate([cos, cos, sin, sin], axis=0))


def _masks():
    k = np.arange(128)[:, None]
    q = np.arange(512)[None, :]
    m = np.stack([(k <= q - 128 * o) for o in range(4)], 0).astype(np.float32)
    return np.ascontiguousarray(m.reshape(4 * 128, 512))


def _prep_shared(ins):
    C = np.ascontiguousarray
    d = {}
    wqkv = ins["L0_attn_W_qkv"]
    blocks = []
    for g in range(8):
        blocks.append(np.concatenate(
            [wqkv[:, g * 128:(g + 1) * 128],
             wqkv[:, 1024 + g * 128:1024 + (g + 1) * 128],
             wqkv[:, 2048 + g * 128:2048 + (g + 1) * 128]], axis=1))
    d["l0_wqkv"] = C(np.stack(blocks, 0))
    bq = ins["L0_attn_b_qkv"]
    d["l0_bqkv"] = C(np.stack(
        [np.concatenate([bq[g * 128:(g + 1) * 128],
                         bq[1024 + g * 128:1024 + (g + 1) * 128],
                         bq[2048 + g * 128:2048 + (g + 1) * 128]])
         for g in range(8)]).reshape(24, 128).T)
    d["l0_wout"] = C(ins["L0_attn_W_out"])
    d["l0_bout"] = C(ins["L0_attn_b_out"][None, :])
    d["l0_rw"] = C(ins["L0_ffn_router_weight"].T)
    d["l0_ebias"] = C(ins["L0_ffn_e_score_bias"][None, :])
    gu = ins["L0_ffn_gate_up_proj"].transpose(0, 2, 1)
    dn = ins["L0_ffn_down_proj"].transpose(0, 2, 1)
    sg = ins["L0_ffn_shared_gate_w"].T
    su = ins["L0_ffn_shared_up_w"].T
    sd = ins["L0_ffn_shared_down_w"].T
    sh_gu = [np.concatenate([sg[:, j * 512:(j + 1) * 512],
                             su[:, j * 512:(j + 1) * 512]], axis=1) for j in range(2)]
    d["gu_all"] = C(np.concatenate([gu, np.stack(sh_gu, 0)], 0))
    sh_dn = [sd[j * 512:(j + 1) * 512, :] for j in range(2)]
    d["dn_all"] = C(np.concatenate([dn, np.stack(sh_dn, 0)], 0))
    wq, wk, wv = ins["L1_attn_Wq"].T, ins["L1_attn_Wk"].T, ins["L1_attn_Wv"].T
    bqv, bkv, bvv = ins["L1_attn_bq"], ins["L1_attn_bk"], ins["L1_attn_bv"]
    blocks, brows = [], []
    for g in range(8):
        ha, hb = 2 * g, 2 * g + 1
        perm = ([ha * 64 + 2 * j for j in range(32)] +
                [hb * 64 + 2 * j for j in range(32)] +
                [ha * 64 + 2 * j + 1 for j in range(32)] +
                [hb * 64 + 2 * j + 1 for j in range(32)])
        vcols = list(range(ha * 64, ha * 64 + 128))
        blocks.append(np.concatenate([wq[:, perm], wk[:, perm], wv[:, vcols]], 1))
        brows.append(np.concatenate([bqv[perm], bkv[perm], bvv[vcols]]))
    d["l1_wqkv"] = C(np.stack(blocks, 0))
    d["l1_bqkv"] = C(np.concatenate(brows).reshape(24, 128).T)
    d["l1_wo"] = C(ins["L1_attn_Wo"].T)
    d["l1_bo"] = C(ins["L1_attn_bo"][None, :])
    d["l1_wT"] = C(ins["L1_ffn_W"].T)
    d["l1_b"] = C(ins["L1_ffn_b"][None, :])
    d["l1_ldT"] = C(ins["L1_ffn_lora_down"].T)
    d["l1_luT"] = C(ins["L1_ffn_lora_up"].T)
    d["l1_da"] = C(ins["L1_ffn_down_aux"])
    d["l1_uaT"] = C(ins["L1_ffn_up_aux"].T)
    d["c_penc"] = _pos_enc()
    d["c_ropecs"] = _rope_cs()
    d["c_masks"] = _masks()
    d["c_ident"] = np.eye(128, dtype=np.float32)
    return d


ENGS = ("sync", "scalar", "vector", "tensor", "gpsimd")


class Em:
    def __init__(self):
        self.streams = {k: [] for k in ENGS}
        self.cnt = {}
        self.lastw = {k: {} for k in ENGS}
        self.sems = {}

    def wait(self, eng, tag):
        if tag is None:
            return
        s, v = tag
        if v <= 0 or self.lastw[eng].get(s, 0) >= v:
            return
        self.lastw[eng][s] = v
        self.streams[eng].append(("wait_ge", {"sem": self.sems[s], "val": v}, None, 0))

    def waits(self, eng, tags):
        for t in tags:
            self.wait(eng, t)

    def dma(self, eng, sem, dst, src):
        self.wait(eng, (sem, self.cnt.get(sem, 0)))
        return self.emit(eng, "dma_start", {"out": dst, "in_": src}, sem=sem, inc=16)

    OWN = {"vector": "ve", "scalar": "se", "gpsimd": "po"}

    def emit(self, eng, meth, kwargs, sem=None, inc=1, chain=None):
        h = None
        v = 0
        own = chain if chain is not None else self.OWN.get(eng)
        if own is not None and meth != "wait_ge":
            self.wait(eng, (own, self.cnt.get(own, 0)))
            if sem is None and meth != "dma_start":
                sem = own
        if sem is not None:
            self.cnt[sem] = self.cnt.get(sem, 0) + inc
            v = self.cnt[sem]
            h = self.sems[sem]
        self.streams[eng].append((meth, kwargs, h, inc))
        return (sem, v) if sem is not None else None

    DMA_SEMS = ("wg", "ws0", "ws1", "wm0", "wm1", "xl0", "xl1",
                "az0", "az1", "ar0", "ar1", "xa")

    def fence(self):
        marks = []
        for eng, sem in (("tensor", "pe"), ("vector", "ve"), ("scalar", "se"),
                         ("gpsimd", "po")):
            pass
        marks0 = [("s2", self.cnt.get("s2", 0))]
        for eng, sem in (("tensor", "pe"), ("vector", "ve"), ("scalar", "se"),
                         ("gpsimd", "po")):
            st = self.streams[eng]
            i = len(st) - 1
            while i >= 0 and st[i][0] in ("wait_ge", "nop"):
                i -= 1
            if i >= 0 and st[i][2] is None:
                self.cnt[sem] = self.cnt.get(sem, 0) + 1
                st[i] = (st[i][0], st[i][1], self.sems[sem], 1)
            marks.append((sem, self.cnt.get(sem, 0)))
        marks += marks0
        for ssem in self.DMA_SEMS:
            marks.append((ssem, self.cnt.get(ssem, 0)))
        for eng in ENGS:
            for t in marks:
                if t is not None and t[0] is not None:
                    self.wait(eng, t)

    def finalize(self, block):
        for name in ENGS:
            items = self.streams[name]
            if not items:
                continue

            def body(e, items=items):
                for meth, kwargs, semh, inc in items:
                    if meth == "wait_ge":
                        e.wait_ge(kwargs["sem"], kwargs["val"])
                        continue
                    inst = getattr(e, meth)(**kwargs)
                    if semh is not None:
                        inst.then_inc(semh, inc)
            getattr(block, name)(body)


class Arena:
    def __init__(self, handle, words):
        self.h = handle
        self.top = 0
        self.words = words
        self.stack = []
        self.peak = 0

    def push(self):
        self.stack.append(self.top)

    def pop(self):
        self.top = self.stack.pop()

    def f32(self, cols):
        base = self.top
        self.top += cols
        self.peak = max(self.peak, self.top)
        assert self.top <= self.words, f"SBUF arena overflow: {self.top}"
        return self.h[:, base:base + cols]

    def bf16(self, cols):
        w = (cols + 1) // 2
        base = self.top
        self.top += w
        self.peak = max(self.peak, self.top)
        assert self.top <= self.words, f"SBUF arena overflow: {self.top}"
        return self.h[:, base:base + w].bitcast(BF16)


def build_program(layers=(0, 1)):
    nc = bass.Bass("TRN2", target_bir_lowering=False)
    in_names = []

    x_in = nc.dram_tensor("x", [TPC, DM], F32, kind="ExternalInput").ap()
    in_names.append("x")
    out = nc.dram_tensor("out", [TPC, DM], F32, kind="ExternalOutput").ap()
    xmid = (nc.dram_tensor("xmid", [TPC, DM], F32, kind="Internal").ap()
            if len(layers) == 2 else None)
    azspill = nc.dram_tensor("azspill", [8, 128, SEQ], BF16, kind="Internal").ap()

    shapes = {
        "l0_wqkv": (8, 1024, 384), "l0_bqkv": (128, 24),
        "l0_wout": (1024, 1024), "l0_bout": (1, 1024),
        "l0_rw": (1024, 32), "l0_ebias": (1, 32),
        "gu_all": (34, 1024, 1024), "dn_all": (34, 512, 1024),
        "l1_wqkv": (8, 1024, 384), "l1_bqkv": (128, 24),
        "l1_wo": (1024, 1024), "l1_bo": (1, 1024),
        "l1_wT": (1024, 1024), "l1_b": (1, 1024),
        "l1_ldT": (100, 4), "l1_luT": (4, 50),
        "l1_da": (100, 1024), "l1_uaT": (50, 1024),
        "c_penc": (128, 2048), "c_ropecs": (128, 2048),
        "c_masks": (512, 512), "c_ident": (128, 128),
    }
    used = set(shapes)
    if 0 not in layers:
        used -= {"l0_wqkv", "l0_bqkv", "l0_wout", "l0_bout", "l0_rw", "l0_ebias",
                 "gu_all", "dn_all", "c_penc"}
    if 1 not in layers:
        used -= {"l1_wqkv", "l1_bqkv", "l1_wo", "l1_bo", "l1_wT", "l1_b",
                 "l1_ldT", "l1_luT", "l1_da", "l1_uaT", "c_ropecs", "c_masks"}
    D = {}
    for k in sorted(used):
        D[k] = nc.dram_tensor(k, list(shapes[k]), F32, kind="ExternalInput").ap()
        in_names.append(k)

    E = Em()
    ctxs = []

    def enter(cm):
        ctxs.append(cm)
        return cm.__enter__()

    arena_h = enter(nc.sbuf_tensor("arena", [128, 52000], F32))
    A = Arena(arena_h, 52000)
    PB = [enter(nc.psum_tensor(f"pb{i}", [128, 512], F32)) for i in range(6)]
    PBbf = [enter(nc.psum_tensor(f"pbb{i}", [128, 1024], BF16)) for i in range(2)]
    PB = PB + [PBbf[0].ap().bitcast(F32) if hasattr(PBbf[0], 'ap') else PBbf[0][:, :].bitcast(F32),
               PBbf[1][:, :].bitcast(F32)]
    for sname in ("wg", "ws0", "ws1", "wm0", "wm1", "xl0", "xl1",
                  "az0", "az1", "ar0", "ar1", "xa",
                  "pe", "ve", "se", "s2", "po", "sp"):
        E.sems[sname] = enter(nc.semaphore(sname))
        E.cnt[sname] = 0
    block = enter(nc.Block())

    def dma_w(dst, src, sem="wg"):
        return E.dma("sync", sem, dst, src)

    def dma_x(dst, src, sem="xa"):
        return E.dma("scalar", sem, dst, src)

    def cast(dst, src, wait_tags=()):
        E.waits("gpsimd", wait_tags)
        return E.emit("gpsimd", "tensor_copy", {"out": dst, "in_": src}, sem="po")

    g_ones_bf = A.bf16(512)
    g_ones_f = A.f32(128)
    g_ident_f = A.f32(128)
    g_ident_bf = A.bf16(128)
    t_id = dma_w(g_ident_f[:, :], D["c_ident"])
    E.emit("gpsimd", "memset", {"ap": g_ones_bf[0:1, :], "constant": 1.0}, sem="po")
    E.emit("gpsimd", "memset", {"ap": g_ones_f[0:1, :], "constant": 1.0}, sem="po")
    E.wait("gpsimd", t_id)
    cast(g_ident_bf[:, :], g_ident_f[:, :])
    E.fence()

    first_layer, last_layer = layers[0], layers[-1]

    for layer in layers:
        src_x = x_in if layer == first_layer else xmid
        dst_x = out if layer == last_layer else xmid
        Bshift = f32(-(B_L0 if layer == 0 else B_L1))

        A.push()
        bqkv_col = A.f32(24)
        bo_bf = A.bf16(1024)
        l_stage = A.f32(2048)
        tc_bqkv = dma_w(bqkv_col[:, :], D["l0_bqkv" if layer == 0 else "l1_bqkv"])
        tb = dma_w(l_stage[0:1, 0:1024], D["l0_bout" if layer == 0 else "l1_bo"])
        tc_bo = cast(bo_bf[0:1, :], l_stage[0:1, 0:1024], (tb,))
        if layer == 1:
            bffn_bf = A.bf16(1024)
            E.wait("sync", tc_bo)
            tb = dma_w(l_stage[0:1, 1024:2048], D["l1_b"])
            tc_bffn = cast(bffn_bf[0:1, :], l_stage[0:1, 1024:2048], (tb,))
        if layer == 0:
            penc_sb = A.f32(2048)
            t_penc = dma_w(penc_sb[:, :], D["c_penc"])
            rwT_f = A.f32(256)
            t_rw = dma_w(rwT_f[:, :].rearrange("p (c e) -> p c e", e=32),
                         D["l0_rw"].rearrange("(c p) e -> p c e", p=128))
            ebias_b = A.f32(32)
            eb_row = A.f32(32)
            t_eb = dma_w(eb_row[0:1, :], D["l0_ebias"])
            E.waits("tensor", (t_eb, ("po", E.cnt["po"])))
            tmm = E.emit("tensor", "matmul",
                         {"out": PB[5][:, 0:32], "lhsT": g_ones_f[0:1, 0:128],
                          "rhs": eb_row[0:1, 0:32], "start": True, "stop": True}, sem="pe")
            E.wait("vector", tmm)
            t_ebb = E.emit("vector", "tensor_copy",
                           {"out": ebias_b[:, :], "in_": PB[5][:, 0:32]}, sem="ve")
        else:
            ropecs_sb = A.f32(2048)
            t_rope = dma_w(ropecs_sb[:, :], D["c_ropecs"])
            masks_bf = A.bf16(2048)
            E.wait("sync", tc_bffn)
            tm = dma_w(l_stage[:, :].rearrange("p (o q) -> p o q", q=512),
                       D["c_masks"].rearrange("(o p) q -> p o q", p=128))
            t_mb = cast(masks_bf[:, :], l_stage[:, :], (tm,))
        E.fence()

        for s in range(NSEQ):
            A.push()
            xn_fm = A.bf16(8 * 2048)
            acc = A.f32(16 * 1024)
            if layer == 0:
                combine = A.f32(16 * 32)

            A.push()
            xstage = A.f32(2 * 2048)
            sq = A.f32(2048)
            st = A.f32(16)
            xn32 = A.f32(2 * 2048)
            if layer == 0:
                xnT = A.f32(1024)
                lgall = A.f32(512)
                r_el = A.f32(512)
                r_rw = A.f32(512)
                r_cor = A.f32(512)
                r_eq = A.f32(512)
                r_sel = A.f32(512)
                r_m = A.f32(64)
            xstage_cons = [None, None]
            xload = [None, None]
            ln_bank = {0: (), 1: ()}
            sq_cons = None
            tr_last = [None, None]
            router_ev = None

            def w2(apv, o=1024):
                return apv.rearrange("p (w o) -> p w o", o=o)

            def bc2(col_ap, o=1024):
                t_ = col_ap.rearrange("p (w o) -> p w o", o=1)
                return bass.AP(t_.tensor, t_.offset,
                               [list(t_.ap[0]), list(t_.ap[1]), [0, o]])

            for bt in range(8):
                r0 = s * SEQ + bt * 256
                par = bt % 2
                if bt == 0:
                    xload[0] = dma_x(xstage[:, 0:2048], src_x[r0:r0 + 256, :]
                                     .rearrange("(w p) o -> p w o", p=128), sem="xl0")
                if bt + 1 < 8:
                    np_ = (bt + 1) % 2
                    E.wait("scalar", xstage_cons[np_])
                    xload[np_] = dma_x(xstage[:, np_ * 2048:(np_ + 1) * 2048]
                                       .rearrange("p (w o) -> p w o", o=1024),
                                       src_x[r0 + 256:r0 + 512, :]
                                       .rearrange("(w p) o -> p w o", p=128),
                                       sem=f"xl{np_}")
                xs = xstage[:, par * 2048:(par + 1) * 2048]
                E.wait("vector", xload[par])
                E.emit("vector", "tensor_reduce",
                       {"out": w2(st[:, 0:2], 1), "in_": w2(xs), "axis": AX.X,
                        "op": ALU.add}, sem="ve")
                E.waits("scalar", (xload[par], sq_cons))
                tsq = E.emit("scalar", "activation",
                             {"out": sq[:, :], "in_": xs, "func": AF.Square}, sem="se")
                E.wait("vector", tsq)
                sq_cons = E.emit("vector", "tensor_reduce",
                                 {"out": w2(st[:, 2:4], 1), "in_": w2(sq[:, :]),
                                  "axis": AX.X, "op": ALU.add}, sem="ve")
                E.emit("vector", "tensor_scalar",
                       {"out": st[:, 4:6], "in0": st[:, 0:2], "scalar1": 1.0 / 1024.0,
                        "scalar2": None, "op0": ALU.mult}, sem="ve")
                E.emit("vector", "tensor_tensor",
                       {"out": st[:, 6:8], "in0": st[:, 4:6], "in1": st[:, 4:6],
                        "op": ALU.mult}, sem="ve")
                tvar = E.emit("vector", "scalar_tensor_tensor",
                              {"out": st[:, 8:10], "in0": st[:, 2:4], "scalar": 1.0 / 1024.0,
                               "in1": st[:, 6:8], "op0": ALU.mult, "op1": ALU.subtract},
                              sem="ve")
                E.wait("scalar", tvar)
                tsd = E.emit("scalar", "activation",
                             {"out": st[:, 10:12], "in_": st[:, 8:10], "func": AF.Sqrt,
                              "bias": f32(1e-5)}, sem="se")
                E.wait("vector", tsd)
                E.emit("vector", "reciprocal", {"out": st[:, 12:14], "in_": st[:, 10:12]},
                       sem="ve")
                xv = xn32[:, par * 2048:(par + 1) * 2048]
                E.wait("vector", tr_last[par])
                E.emit("vector", "tensor_tensor",
                       {"out": w2(xv), "in0": w2(xs), "in1": bc2(st[:, 4:6]),
                        "op": ALU.subtract}, sem="ve")
                txn = E.emit("vector", "tensor_tensor",
                             {"out": w2(xv), "in0": w2(xv), "in1": bc2(st[:, 12:14]),
                              "op": ALU.mult}, sem="ve")
                xstage_cons[par] = E.emit("vector", "tensor_copy",
                       {"out": acc[:, bt * 2048:(bt + 1) * 2048], "in_": xs}, sem="ve")
                for sub in range(2):
                    tt = bt * 2 + sub
                    for c in range(8 if "notr" not in PHASES else 0):
                        bk = c % 2
                        E.wait("tensor", txn)
                        E.waits("tensor", ln_bank[bk])
                        ttr = E.emit("tensor", "matmul",
                                     {"out": PB[bk][:, 0:128],
                                      "lhsT": xv[:, sub * 1024 + c * 128: sub * 1024 + (c + 1) * 128],
                                      "rhs": g_ident_f[:, :], "start": True, "stop": True},
                                     sem="pe")
                        E.wait("scalar", ttr)
                        tev1 = E.emit("scalar", "copy",
                                      {"out": xn_fm[:, c * 2048 + tt * 128: c * 2048 + (tt + 1) * 128],
                                       "in_": PB[bk][:, 0:128]}, sem="se")
                        if layer == 0:
                            E.waits("vector", (ttr, tev1))
                            tev2 = E.emit("vector", "tensor_copy",
                                          {"out": xnT[:, c * 128:(c + 1) * 128],
                                           "in_": PB[bk][:, 0:128]}, sem="ve")
                            ln_bank[bk] = (tev1, tev2)
                        else:
                            ln_bank[bk] = (tev1,)
                    if layer == 0 and "norouter" not in PHASES and "notr" not in PHASES:
                        E.waits("tensor", (t_rw, router_ev) + ln_bank[0] + ln_bank[1])
                        tag = None
                        for c in range(8):
                            tag = E.emit("tensor", "matmul",
                                         {"out": PB[5][:, 0:32],
                                          "lhsT": xnT[:, c * 128:(c + 1) * 128],
                                          "rhs": rwT_f[:, c * 32:(c + 1) * 32],
                                          "start": c == 0, "stop": c == 7},
                                         sem="pe" if c == 7 else None)
                        E.wait("vector", tag)
                        router_ev = E.emit("vector", "tensor_copy",
                                           {"out": lgall[:, tt * 32:(tt + 1) * 32],
                                            "in_": PB[5][:, 0:32]}, sem="ve")
                tr_last[par] = ("pe", E.cnt["pe"])

            if layer == 0 and "norouter" not in PHASES and "notr" not in PHASES:
                def w3(apv, o=32):
                    return apv.rearrange("p (w o) -> p w o", o=o)

                def bcast16(col_ap):
                    t_ = col_ap.rearrange("p (w o) -> p w o", o=1)
                    return bass.AP(t_.tensor, t_.offset,
                                   [list(t_.ap[0]), list(t_.ap[1]), [0, 32]])

                mrow = r_m[:, 0:16]
                E.emit("vector", "tensor_reduce",
                       {"out": w3(mrow, 1), "in_": w3(lgall), "axis": AX.X,
                        "op": ALU.max}, sem="ve")
                tsub = E.emit("vector", "tensor_tensor",
                              {"out": w3(r_el), "in0": w3(lgall), "in1": bcast16(mrow),
                               "op": ALU.subtract}, sem="ve")
                E.wait("scalar", tsub)
                tex = E.emit("scalar", "activation",
                             {"out": r_el[:, :], "in_": r_el[:, :], "func": AF.Exp},
                             sem="se")
                E.wait("vector", tex)
                smrow = r_m[:, 16:32]
                E.emit("vector", "tensor_reduce",
                       {"out": w3(smrow, 1), "in_": w3(r_el), "axis": AX.X,
                        "op": ALU.add}, sem="ve")
                E.emit("vector", "reciprocal", {"out": smrow, "in_": smrow}, sem="ve")
                E.emit("vector", "tensor_tensor",
                       {"out": w3(r_rw), "in0": w3(r_el), "in1": bcast16(smrow),
                        "op": ALU.mult}, sem="ve")
                E.wait("vector", t_ebb)
                eb3 = bass.AP(ebias_b.tensor, ebias_b.offset,
                              [list(ebias_b.ap[0]), [0, 16], [1, 32]])
                E.emit("vector", "tensor_tensor",
                       {"out": w3(r_cor), "in0": w3(r_rw), "in1": eb3, "op": ALU.add},
                       sem="ve")
                E.emit("vector", "memset", {"ap": r_sel[:, :], "constant": 0.0}, sem="ve")
                m2row = r_m[:, 32:48]
                for _ in range(6):
                    E.emit("vector", "tensor_reduce",
                           {"out": w3(m2row, 1), "in_": w3(r_cor), "axis": AX.X,
                            "op": ALU.max}, sem="ve")
                    E.emit("vector", "tensor_tensor",
                           {"out": w3(r_eq), "in0": w3(r_cor), "in1": bcast16(m2row),
                            "op": ALU.is_ge}, sem="ve")
                    E.emit("vector", "tensor_add",
                           {"out": r_sel[:, :], "in0": r_sel[:, :], "in1": r_eq[:, :]},
                           sem="ve")
                    E.emit("vector", "scalar_tensor_tensor",
                           {"out": r_cor[:, :], "in0": r_eq[:, :], "scalar": -1e9,
                            "in1": r_cor[:, :], "op0": ALU.mult, "op1": ALU.add}, sem="ve")
                E.emit("vector", "tensor_mul",
                       {"out": r_eq[:, :], "in0": r_rw[:, :], "in1": r_sel[:, :]}, sem="ve")
                tsrow = r_m[:, 48:64]
                E.emit("vector", "tensor_reduce",
                       {"out": w3(tsrow, 1), "in_": w3(r_eq), "axis": AX.X,
                        "op": ALU.add}, sem="ve")
                E.emit("vector", "tensor_scalar_max",
                       {"out": tsrow, "in0": tsrow, "scalar1": 1e-12}, sem="ve")
                E.emit("vector", "reciprocal", {"out": tsrow, "in_": tsrow}, sem="ve")
                E.emit("vector", "tensor_tensor",
                       {"out": w3(combine[:, :]), "in0": w3(r_eq), "in1": bcast16(tsrow),
                        "op": ALU.mult}, sem="ve")
            E.fence()
            A.pop()

            A.push()
            do_attn = "attn" in PHASES
            wstage = A.f32(2 * 768)
            wbf = A.bf16(2 * 3072)
            vg = A.bf16(2048)
            vT = A.bf16(16 * 130)
            probs = A.bf16(16 * 512)
            az = A.bf16(2 * 2048)
            zf = A.f32(512)
            rbrow = A.f32(2 * 512)
            rb_bf = A.bf16(2 * 512)
            if layer == 0:
                qg = A.bf16(2048)
                kg = A.bf16(2048)
                q_src, k_src = qg, kg
            else:
                qh = A.bf16(2048)
                kh = A.bf16(2048)
                sc2r = A.f32(1024)
                q_src, k_src = qh, kh

            wq_d = D["l0_wqkv" if layer == 0 else "l1_wqkv"]
            wload = [None, None]
            wcast = [None, None]
            w_cons = [None, None]
            probs_cons = [None] * 16
            az_cons = [None, None]
            qkv_cons = None
            vT_cons = None
            bank_ev = {0: None, 1: None}
            zb_ev = [(), ()]
            bc_ev = None

            def load_group(g):
                par = g % 2
                E.wait("sync", w_cons[par])
                src3 = wq_d[g].rearrange("(c p) j -> p c j", p=128)
                tcc = None
                for hp in range(4):
                    E.wait("sync", tcc)
                    tld = dma_w(wstage[:, par * 768:(par + 1) * 768]
                                .rearrange("p (c j) -> p c j", j=384),
                                src3[:, 2 * hp:2 * hp + 2], sem=f"ws{par}")
                    tcc = cast(wbf[:, par * 3072 + hp * 768: par * 3072 + (hp + 1) * 768],
                               wstage[:, par * 768:(par + 1) * 768], (tld,))
                wload[par] = tcc
                wcast[par] = tcc

            if do_attn:
                load_group(0)
            for g in range(8 if do_attn else 0):
                par = g % 2
                if g + 1 < 8:
                    load_group(g + 1)
                wv = wbf[:, par * 3072:(par + 1) * 3072]
                tms = {}
                for p in range(3):
                    for tcn in range(4):
                        bk = (p * 4 + tcn) % 2
                        E.waits("tensor", (wcast[par], bank_ev[bk]))
                        for c in range(8):
                            tmm = E.emit("tensor", "matmul",
                                   {"out": PB[bk][:, :],
                                    "lhsT": wv[:, c * 384 + p * 128: c * 384 + (p + 1) * 128],
                                    "rhs": xn_fm[:, c * 2048 + tcn * 512: c * 2048 + (tcn + 1) * 512],
                                    "start": c == 0, "stop": c == 7},
                                   sem="pe" if c == 7 else None)
                        bcol = bqkv_col[:, g * 3 + p: g * 3 + p + 1]
                        if layer == 0:
                            if p == 0:
                                E.waits("vector", (tmm, qkv_cons, tc_bqkv))
                                tev = E.emit("vector", "tensor_scalar",
                                             {"out": qg[:, tcn * 512:(tcn + 1) * 512],
                                              "in0": PB[bk][:, :], "scalar1": bcol,
                                              "scalar2": None, "op0": ALU.add}, sem="ve")
                            elif p == 1:
                                E.waits("vector", (tmm, qkv_cons, t_penc, tc_bqkv))
                                tev = E.emit("vector", "scalar_tensor_tensor",
                                             {"out": kg[:, tcn * 512:(tcn + 1) * 512],
                                              "in0": PB[bk][:, :], "scalar": bcol,
                                              "in1": penc_sb[:, tcn * 512:(tcn + 1) * 512],
                                              "op0": ALU.add, "op1": ALU.add}, sem="ve")
                            else:
                                E.waits("vector", (tmm, vT_cons, tc_bqkv))
                                tev = E.emit("vector", "tensor_scalar",
                                             {"out": vg[:, tcn * 512:(tcn + 1) * 512],
                                              "in0": PB[bk][:, :], "scalar1": bcol,
                                              "scalar2": None, "op0": ALU.add}, sem="ve")
                            bank_ev[bk] = tev
                        else:
                            if p < 2:
                                dst = qh if p == 0 else kh
                                if tcn == 0:
                                    E.waits("gpsimd", (qkv_cons,))
                                    tms[p] = E.emit("gpsimd", "memset",
                                                    {"ap": dst[:, :], "constant": 0.0}, sem="po")
                                E.waits("vector", (tmm, tms[p], t_rope, qkv_cons, tc_bqkv))
                                E.emit("vector", "scalar_tensor_tensor",
                                       {"out": sc2r[0:64, 0:512], "in0": PB[bk][0:64, :],
                                        "scalar": bcol[0:64, :],
                                        "in1": ropecs_sb[0:64, tcn * 512:(tcn + 1) * 512],
                                        "op0": ALU.add, "op1": ALU.mult}, sem="ve")
                                E.emit("vector", "scalar_tensor_tensor",
                                       {"out": sc2r[0:64, 512:1024], "in0": PB[bk][64:128, :],
                                        "scalar": bcol[64:128, :],
                                        "in1": ropecs_sb[64:128, tcn * 512:(tcn + 1) * 512],
                                        "op0": ALU.add, "op1": ALU.mult}, sem="ve")
                                E.emit("vector", "tensor_sub",
                                       {"out": dst[0:32, tcn * 512:(tcn + 1) * 512],
                                        "in0": sc2r[0:32, 0:512], "in1": sc2r[0:32, 512:1024]},
                                       sem="ve")
                                tev = E.emit("vector", "tensor_sub",
                                             {"out": dst[64:96, tcn * 512:(tcn + 1) * 512],
                                              "in0": sc2r[32:64, 0:512],
                                              "in1": sc2r[32:64, 512:1024]}, sem="ve")
                            else:
                                E.waits("vector", (tmm, vT_cons, tc_bqkv))
                                tev = E.emit("vector", "tensor_scalar",
                                             {"out": vg[:, tcn * 512:(tcn + 1) * 512],
                                              "in0": PB[bk][:, :], "scalar1": bcol,
                                              "scalar2": None, "op0": ALU.add}, sem="ve")
                            bank_ev[bk] = tev

                E.waits("gpsimd", (vT_cons,))
                tvo = E.emit("gpsimd", "memset", {"ap": vT[:, :], "constant": 1.0}, sem="po")
                vt_bank = {0: None, 1: None}
                last_vt_ev = None
                for kt in range(16):
                    bb = kt % 2
                    E.waits("tensor", (bank_ev[0], bank_ev[1], vt_bank[bb]))
                    ttr = E.emit("tensor", "transpose",
                                 {"out": PBbf[bb][:, 0:128],
                                  "in_": vg[:, kt * 128:(kt + 1) * 128],
                                  "identity": g_ident_bf[:, :]}, sem="pe")
                    E.waits("vector", (ttr, tvo))
                    tev = E.emit("vector", "tensor_copy",
                                 {"out": vT[:, kt * 130:kt * 130 + 130].rearrange(
                                     "p (h d) -> p h d", h=2, d=65)[:, :, 0:64],
                                  "in_": PBbf[bb][:, 0:128].rearrange(
                                      "p (h d) -> p h d", h=2, d=64)}, sem="ve")
                    vt_bank[bb] = tev
                    last_vt_ev = tev

                def emit_epilogue(pend):
                    nonlocal bc_ev
                    h, qc, zb, zbank, av_last, trb, trc = pend
                    hsl = slice(h * 64, (h + 1) * 64)
                    E.waits("tensor", (trb, bc_ev))
                    tbc = E.emit("tensor", "matmul",
                                 {"out": PB[4][0:64, :], "lhsT": g_ones_bf[0:1, 0:64],
                                  "rhs": rb_bf[0:1, zbank * 512:zbank * 512 + 512],
                                  "start": True, "stop": True,
                                  "skip_group_check": True}, sem="pe")
                    bc_done[zbank] = tbc
                    E.waits("scalar", (av_last, bc_ev, trc))
                    tze = E.emit("scalar", "copy",
                                 {"out": zf[0:64, :], "in_": zb[0:64, :]}, sem="se")
                    zb_ev[zbank] = (tze, trc)
                    E.waits("vector", (tze, tbc, az_cons[par]))
                    bc_ev = E.emit("vector", "tensor_tensor",
                                   {"out": az[hsl, par * 2048 + qc * 512:
                                              par * 2048 + (qc + 1) * 512],
                                    "in0": zf[0:64, :], "in1": PB[4][0:64, :],
                                    "op": ALU.mult}, sem="ve")

                tiles = [(h, qc) for h in range(2) for qc in range(4)]
                readyq = {}
                slot_of = {}
                slot_rr = [0]

                def emit_scores(ti):
                    h, qc = tiles[ti]
                    hsl = slice(h * 64, (h + 1) * 64)
                    ktmax = 16 if layer == 0 else 4 * qc + 4
                    readyq[ti] = [None] * ktmax
                    for kt in range(ktmax):
                        sbk = kt % 2
                        slot = slot_rr[0]
                        slot_rr[0] = (slot + 1) % 16
                        slot_of[(ti, kt)] = slot
                        E.waits("tensor", (bank_ev[sbk], last_vt_ev))
                        tsc = E.emit("tensor", "matmul",
                                     {"out": PB[sbk][:, :],
                                      "lhsT": k_src[hsl, kt * 128:(kt + 1) * 128],
                                      "rhs": q_src[hsl, qc * 512:(qc + 1) * 512],
                                      "start": True, "stop": True,
                                      "skip_group_check": True}, sem="pe")
                        E.waits("scalar", (tsc, probs_cons[slot]))
                        texp = E.emit("scalar", "activation",
                                      {"out": probs[:, slot * 512:(slot + 1) * 512],
                                       "in_": PB[sbk][:, :], "func": AF.Exp,
                                       "bias": Bshift, "scale": f32(0.125)},
                                      chain="se" if sbk == 0 else "s2")
                        bank_ev[sbk] = texp
                        readyq[ti][kt] = texp
                        if layer == 1 and kt >= 4 * qc:
                            o = kt - 4 * qc
                            E.waits("vector", (texp, t_mb, probs_cons[slot]))
                            readyq[ti][kt] = E.emit("vector", "tensor_tensor",
                                           {"out": probs[:, slot * 512:(slot + 1) * 512],
                                            "in0": probs[:, slot * 512:(slot + 1) * 512],
                                            "in1": masks_bf[:, o * 512:(o + 1) * 512],
                                            "op": ALU.mult}, sem="ve")

                def emit_av(ti, kt, zb, ktmax):
                    h, qc = tiles[ti]
                    slot = slot_of[(ti, kt)]
                    E.wait("tensor", readyq[ti][kt])
                    tav = E.emit("tensor", "matmul",
                                 {"out": zb[0:65, :],
                                  "lhsT": vT[:, kt * 130 + h * 65: kt * 130 + (h + 1) * 65],
                                  "rhs": probs[:, slot * 512:(slot + 1) * 512],
                                  "start": kt == 0, "stop": kt == ktmax - 1,
                                  "skip_group_check": True}, sem="pe")
                    probs_cons[slot] = tav
                    return tav

                zbank = 0
                pends = []
                bc_done = [None, None]
                emit_scores(0)
                for ti in range(len(tiles)):
                    h, qc = tiles[ti]
                    ktmax = 16 if layer == 0 else 4 * qc + 4
                    zb = PB[2 + zbank]
                    if len(pends) >= 2:
                        emit_epilogue(pends.pop(0))
                    E.waits("tensor", zb_ev[zbank])
                    av_last = None
                    if ti + 1 < len(tiles):
                        h2, qc2 = tiles[ti + 1]
                        ktmax2 = 16 if layer == 0 else 4 * qc2 + 4
                        hsl2 = slice(h2 * 64, (h2 + 1) * 64)
                        readyq[ti + 1] = [None] * ktmax2
                        for kt in range(max(ktmax, ktmax2)):
                            if kt < ktmax:
                                av_last = emit_av(ti, kt, zb, ktmax)
                            if kt < ktmax2:
                                sbk = kt % 2
                                slot = slot_rr[0]
                                slot_rr[0] = (slot + 1) % 16
                                slot_of[(ti + 1, kt)] = slot
                                E.waits("tensor", (bank_ev[sbk],))
                                tsc = E.emit("tensor", "matmul",
                                             {"out": PB[sbk][:, :],
                                              "lhsT": k_src[hsl2, kt * 128:(kt + 1) * 128],
                                              "rhs": q_src[hsl2, qc2 * 512:(qc2 + 1) * 512],
                                              "start": True, "stop": True,
                                              "skip_group_check": True}, sem="pe")
                                E.waits("scalar", (tsc, probs_cons[slot]))
                                texp = E.emit("scalar", "activation",
                                              {"out": probs[:, slot * 512:(slot + 1) * 512],
                                               "in_": PB[sbk][:, :], "func": AF.Exp,
                                               "bias": Bshift, "scale": f32(0.125)},
                                              chain="se" if sbk == 0 else "s2")
                                bank_ev[sbk] = texp
                                readyq[ti + 1][kt] = texp
                                if layer == 1 and kt >= 4 * qc2:
                                    o = kt - 4 * qc2
                                    E.waits("vector", (texp, t_mb, probs_cons[slot]))
                                    readyq[ti + 1][kt] = E.emit("vector", "tensor_tensor",
                                                   {"out": probs[:, slot * 512:(slot + 1) * 512],
                                                    "in0": probs[:, slot * 512:(slot + 1) * 512],
                                                    "in1": masks_bf[:, o * 512:(o + 1) * 512],
                                                    "op": ALU.mult}, sem="ve")
                    else:
                        for kt in range(ktmax):
                            av_last = emit_av(ti, kt, zb, ktmax)
                    vT_cons = av_last
                    qkv_cons = ("pe", E.cnt["pe"])
                    E.waits("vector", (av_last, bc_done[zbank]))
                    trc = E.emit("vector", "reciprocal",
                                 {"out": rbrow[0:1, zbank * 512:zbank * 512 + 512],
                                  "in_": zb[64:65, :]}, sem="ve")
                    trb = E.emit("vector", "tensor_copy",
                                 {"out": rb_bf[0:1, zbank * 512:zbank * 512 + 512],
                                  "in_": rbrow[0:1, zbank * 512:zbank * 512 + 512]}, sem="ve")
                    pends.append((h, qc, zb, zbank, av_last, trb, trc))
                    zbank ^= 1
                for p_ in pends:
                    emit_epilogue(p_)
                pends = []
                E.wait("scalar", bc_ev)
                az_cons[par] = dma_x(azspill[g], az[:, par * 2048:(par + 1) * 2048],
                                     sem=f"az{par}")
                w_cons[par] = ("pe", E.cnt["pe"])
            E.fence()
            A.pop()

            A.push()
            do_wout = ("wout" in PHASES) and ("attn" in PHASES)
            wo_stage = A.f32(4096)
            wo_bf = A.bf16(8192)
            azr = A.bf16(2 * 1024)
            wo_d = D["l0_wout" if layer == 0 else "l1_wo"]
            if not do_wout:
                tl1 = tcw1 = tl2 = tcw2 = None
            tl1 = dma_w(wo_stage[:, :].rearrange("p (c o) -> p c o", o=1024),
                        wo_d[0:512].rearrange("(c p) o -> p c o", p=128))
            tcw1 = cast(wo_bf[:, 0:4096], wo_stage[:, :], (tl1,))
            E.wait("sync", tcw1)
            tl2 = dma_w(wo_stage[:, :].rearrange("p (c o) -> p c o", o=1024),
                        wo_d[512:1024].rearrange("(c p) o -> p c o", p=128))
            tcw2 = cast(wo_bf[:, 4096:8192], wo_stage[:, :], (tl2,))
            azld = [None, None]
            az_fr = [None, None]
            wo_bank = {0: None, 1: None}
            for tt in range(16 if do_wout else 0):
                par = tt % 2
                E.wait("scalar", az_fr[par])
                azld[par] = dma_x(
                    azr[:, par * 1024:(par + 1) * 1024].rearrange("p (c j) -> p c j", j=128),
                    azspill[:, :, tt * 128:(tt + 1) * 128].transpose([1, 0, 2]),
                    sem=f"ar{par}")
                for oc in range(2):
                    bk = oc
                    E.waits("tensor", (azld[par], tcw1, tcw2, wo_bank[bk], tc_bo))
                    for c in range(8):
                        E.emit("tensor", "matmul",
                               {"out": PB[bk][:, :],
                                "lhsT": azr[:, par * 1024 + c * 128: par * 1024 + (c + 1) * 128],
                                "rhs": wo_bf[:, c * 1024 + oc * 512: c * 1024 + (oc + 1) * 512],
                                "start": c == 0, "stop": False}, sem=None)
                    tmm = E.emit("tensor", "matmul",
                                 {"out": PB[bk][:, :], "lhsT": g_ones_bf[0:1, 0:128],
                                  "rhs": bo_bf[0:1, oc * 512:(oc + 1) * 512],
                                  "start": False, "stop": True}, sem="pe")
                    E.wait("vector", tmm)
                    tev = E.emit("vector", "tensor_add",
                                 {"out": acc[:, tt * 1024 + oc * 512: tt * 1024 + (oc + 1) * 512],
                                  "in0": acc[:, tt * 1024 + oc * 512: tt * 1024 + (oc + 1) * 512],
                                  "in1": PB[bk][:, :]}, sem="ve")
                    wo_bank[bk] = tev
                az_fr[par] = ("pe", E.cnt["pe"])
            E.fence()
            A.pop()

            A.push()
            if "ffn" not in PHASES:
                pass
            elif layer == 0:
                gstage = A.f32(2 * 2048)
                gu_bf = A.bf16(2 * 8192)
                dn_bf = A.bf16(2 * 4096)
                hmid = A.bf16(2048)
                silu_sb = A.bf16(512)
                gcast = [None, None]
                dcast = [None, None]
                g_cons = [None, None]
                gu_bank = {0: None, 1: None, 2: None, 3: None}
                dn_bank = {4: None, 5: None, 6: None, 7: None}
                hm_cons = None
                silu_cons = None

                def load_expert(e_idx):
                    par = e_idx % 2
                    E.wait("sync", g_cons[par])
                    for j in range(4):
                        sp = j % 2
                        t = dma_w(gstage[:, sp * 2048:(sp + 1) * 2048]
                                  .rearrange("p (c o) -> p c o", o=1024),
                                  D["gu_all"][e_idx, 2 * j * 128:(2 * j + 2) * 128, :]
                                  .rearrange("(c p) o -> p c o", p=128), sem=f"wm{sp}")
                        tcc = cast(gu_bf[:, par * 8192 + 2048 * j: par * 8192 + 2048 * (j + 1)],
                                   gstage[:, sp * 2048:(sp + 1) * 2048], (t,))
                        E.wait("sync", tcc)
                        if j == 3:
                            gcast[par] = tcc
                    for j in range(2):
                        sp = j % 2
                        t = dma_w(gstage[:, sp * 2048:(sp + 1) * 2048]
                                  .rearrange("p (c o) -> p c o", o=1024),
                                  D["dn_all"][e_idx, 2 * j * 128:(2 * j + 2) * 128, :]
                                  .rearrange("(c p) o -> p c o", p=128), sem=f"wm{sp}")
                        tcc = cast(dn_bf[:, par * 4096 + 2048 * j: par * 4096 + 2048 * (j + 1)],
                                   gstage[:, sp * 2048:(sp + 1) * 2048], (t,))
                        E.wait("sync", tcc)
                        if j == 1:
                            dcast[par] = tcc

                load_expert(0)
                for e_idx in range(34):
                    par = e_idx % 2
                    if e_idx + 1 < 34:
                        load_expert(e_idx + 1)
                    guv = gu_bf[:, par * 8192:(par + 1) * 8192]
                    dnv = dn_bf[:, par * 4096:(par + 1) * 4096]
                    for tb in range(4):
                        hm_ev = None
                        for j in range(4):
                            bg = (j % 2) * 2
                            bu = bg + 1
                            E.waits("tensor", (gcast[par], gu_bank[bg]))
                            for c in range(8):
                                E.emit("tensor", "matmul",
                                       {"out": PB[bg][:, :],
                                        "lhsT": guv[:, c * 1024 + j * 128: c * 1024 + (j + 1) * 128],
                                        "rhs": xn_fm[:, c * 2048 + tb * 512: c * 2048 + (tb + 1) * 512],
                                        "start": c == 0, "stop": c == 7},
                                       sem="pe" if c == 7 else None)
                            tg = ("pe", E.cnt["pe"])
                            E.waits("tensor", (gu_bank[bu],))
                            for c in range(8):
                                E.emit("tensor", "matmul",
                                       {"out": PB[bu][:, :],
                                        "lhsT": guv[:, c * 1024 + (j + 4) * 128: c * 1024 + (j + 5) * 128],
                                        "rhs": xn_fm[:, c * 2048 + tb * 512: c * 2048 + (tb + 1) * 512],
                                        "start": c == 0, "stop": c == 7},
                                       sem="pe" if c == 7 else None)
                            tu = ("pe", E.cnt["pe"])
                            E.waits("scalar", (tg, silu_cons))
                            tsi = E.emit("scalar", "activation",
                                         {"out": silu_sb[:, :], "in_": PB[bg][:, :],
                                          "func": AF.Silu}, sem="se")
                            gu_bank[bg] = tsi
                            E.waits("vector", (tsi, tu, hm_cons))
                            thm = E.emit("vector", "tensor_tensor",
                                         {"out": hmid[:, j * 512:(j + 1) * 512],
                                          "in0": silu_sb[:, :], "in1": PB[bu][:, :],
                                          "op": ALU.mult}, sem="ve")
                            gu_bank[bu] = thm
                            hm_ev = thm
                            silu_cons = thm
                        for tt2 in range(4):
                            tt_g = tb * 4 + tt2
                            for oc in range(2):
                                bk = 4 + ((tt2 * 2 + oc) % 4)
                                E.waits("tensor", (hm_ev, dcast[par], dn_bank[bk]))
                                for ic in range(4):
                                    E.emit("tensor", "matmul",
                                           {"out": PB[bk][:, :],
                                            "lhsT": hmid[:, ic * 512 + tt2 * 128:
                                                         ic * 512 + (tt2 + 1) * 128],
                                            "rhs": dnv[:, ic * 1024 + oc * 512:
                                                       ic * 1024 + (oc + 1) * 512],
                                            "start": ic == 0, "stop": ic == 3},
                                           sem="pe" if ic == 3 else None)
                                tdn = ("pe", E.cnt["pe"])
                                E.wait("vector", tdn)
                                asl = acc[:, tt_g * 1024 + oc * 512: tt_g * 1024 + (oc + 1) * 512]
                                if e_idx < 32:
                                    tev = E.emit("vector", "scalar_tensor_tensor",
                                                 {"out": asl, "in0": PB[bk][:, :],
                                                  "scalar": combine[:, tt_g * 32 + e_idx:
                                                                    tt_g * 32 + e_idx + 1],
                                                  "in1": asl, "op0": ALU.mult, "op1": ALU.add},
                                                 sem="ve")
                                else:
                                    tev = E.emit("vector", "tensor_add",
                                                 {"out": asl, "in0": asl, "in1": PB[bk][:, :]},
                                                 sem="ve")
                                dn_bank[bk] = tev
                        hm_cons = ("pe", E.cnt["pe"])
                    g_cons[par] = ("pe", E.cnt["pe"])
            else:
                sm = A.f32(2048)
                ld_bf = A.bf16(8)
                lu_bf = A.bf16(64)
                da_bf = A.bf16(1024)
                ua_bf = A.bf16(1024)
                m1_bf = A.bf16(1024)
                m2_bf = A.bf16(1024)
                weff = A.bf16(8192)
                wstage2 = A.f32(2 * 1024)
                t1_ = dma_w(sm[0:100, 0:4], D["l1_ldT"])
                c1_ = cast(ld_bf[0:100, 0:4], sm[0:100, 0:4], (t1_,))
                E.wait("sync", c1_)
                t2_ = dma_w(sm[0:4, 4:54], D["l1_luT"])
                c2_ = cast(lu_bf[0:4, 0:50], sm[0:4, 4:54], (t2_,))
                E.wait("sync", c2_)
                t3_ = dma_w(sm[0:100, 0:1024], D["l1_da"])
                c3_ = cast(da_bf[0:100, :], sm[0:100, 0:1024], (t3_,))
                t4_ = dma_w(sm[0:50, 1024:2048], D["l1_uaT"])
                c4_ = cast(ua_bf[0:50, :], sm[0:50, 1024:2048], (t4_,))
                ce = None
                for j in range(2):
                    E.waits("tensor", (c1_, c3_, ce))
                    tmj = E.emit("tensor", "matmul",
                                 {"out": PB[4][0:4, :], "lhsT": ld_bf[0:100, 0:4],
                                  "rhs": da_bf[0:100, j * 512:(j + 1) * 512],
                                  "start": True, "stop": True}, sem="pe")
                    E.wait("scalar", tmj)
                    ce = E.emit("scalar", "copy",
                                {"out": m1_bf[0:4, j * 512:(j + 1) * 512],
                                 "in_": PB[4][0:4, :]}, sem="se")
                ce2 = None
                for j in range(2):
                    E.waits("tensor", (c2_, ce, ce2))
                    tmj = E.emit("tensor", "matmul",
                                 {"out": PB[5][0:50, :], "lhsT": lu_bf[0:4, 0:50],
                                  "rhs": m1_bf[0:4, j * 512:(j + 1) * 512],
                                  "start": True, "stop": True}, sem="pe")
                    E.wait("scalar", tmj)
                    ce2 = E.emit("scalar", "copy",
                                 {"out": m2_bf[0:50, j * 512:(j + 1) * 512],
                                  "in_": PB[5][0:50, :]}, sem="se")
                wld = [None, None]
                weff_bank = {4: None, 5: None}
                wstage_cons = [None, None]
                for ic in range(8):
                    par2 = ic % 2
                    E.wait("sync", wstage_cons[par2])
                    wld[par2] = dma_w(wstage2[:, par2 * 1024:(par2 + 1) * 1024],
                                      D["l1_wT"][ic * 128:(ic + 1) * 128, :], sem=f"ws{par2}")
                    for oh in range(2):
                        bk = 4 + oh
                        E.waits("tensor", (ce2, c4_, weff_bank[bk]))
                        tmj = E.emit("tensor", "matmul",
                                     {"out": PB[bk][:, :],
                                      "lhsT": m2_bf[0:50, ic * 128:(ic + 1) * 128],
                                      "rhs": ua_bf[0:50, oh * 512:(oh + 1) * 512],
                                      "start": True, "stop": True}, sem="pe")
                        E.waits("vector", (tmj, wld[par2]))
                        tev = E.emit("vector", "scalar_tensor_tensor",
                                     {"out": weff[:, ic * 1024 + oh * 512:
                                                  ic * 1024 + (oh + 1) * 512],
                                      "in0": PB[bk][:, :], "scalar": 1.0,
                                      "in1": wstage2[:, par2 * 1024 + oh * 512:
                                                     par2 * 1024 + (oh + 1) * 512],
                                      "op0": ALU.mult, "op1": ALU.add}, sem="ve")
                        weff_bank[bk] = tev
                    wstage_cons[par2] = ("ve", E.cnt["ve"])
                tweff = ("ve", E.cnt["ve"])
                ffn_bank = {0: None, 1: None}
                for tt in range(16):
                    for oc in range(2):
                        bk = oc
                        E.waits("tensor", (tweff, ffn_bank[bk], tc_bffn))
                        for c in range(8):
                            E.emit("tensor", "matmul",
                                   {"out": PB[bk][:, :],
                                    "lhsT": xn_fm[:, c * 2048 + tt * 128: c * 2048 + (tt + 1) * 128],
                                    "rhs": weff[:, c * 1024 + oc * 512: c * 1024 + (oc + 1) * 512],
                                    "start": c == 0, "stop": False}, sem=None)
                        tmm = E.emit("tensor", "matmul",
                                     {"out": PB[bk][:, :], "lhsT": g_ones_bf[0:1, 0:128],
                                      "rhs": bffn_bf[0:1, oc * 512:(oc + 1) * 512],
                                      "start": False, "stop": True}, sem="pe")
                        E.wait("vector", tmm)
                        tev = E.emit("vector", "tensor_add",
                                     {"out": acc[:, tt * 1024 + oc * 512:
                                                 tt * 1024 + (oc + 1) * 512],
                                      "in0": acc[:, tt * 1024 + oc * 512:
                                                 tt * 1024 + (oc + 1) * 512],
                                      "in1": PB[bk][:, :]}, sem="ve")
                        ffn_bank[bk] = tev
            E.fence()
            A.pop()

            for tt in range(16):
                r0 = s * SEQ + tt * 128
                dma_x(dst_x[r0:r0 + 128, :], acc[:, tt * 1024:(tt + 1) * 1024])
            E.fence()
            A.pop()
        A.pop()

    E.wait("sync", ("xa", E.cnt.get("xa", 0)))
    E.finalize(block)
    for cm in reversed(ctxs):
        cm.__exit__(None, None, None)
    n_inst = sum(len(v) for v in E.streams.values())
    return nc, in_names, n_inst, A.peak


_cache = {}


def _get_program(layers):
    key = tuple(layers)
    if key not in _cache:
        _cache[key] = build_program(layers)
    return _cache[key]


def kernel(**inputs):
    shared = _prep_shared(inputs)
    x = np.ascontiguousarray(np.asarray(inputs["L0_attn_x"], dtype=np.float32)
                             .reshape(16, 2048, 1024))
    xs = [np.ascontiguousarray(x[2 * c:2 * c + 2].reshape(TPC, DM)) for c in range(NCORES)]

    def maps(names, xlist):
        return [{n: (xlist[c] if n == "x" else shared[n]) for n in names}
                for c in range(NCORES)]

    if SPLIT_LAYERS:
        nc0, names0, _, _ = _get_program((0,))
        r0 = run_bass_kernel_spmd(nc0, maps(names0, xs), core_ids=list(range(NCORES)))
        x1s = [np.ascontiguousarray(np.asarray(r["out"])) for r in r0.results]
        nc1, names1, _, _ = _get_program((1,))
        r1 = run_bass_kernel_spmd(nc1, maps(names1, x1s), core_ids=list(range(NCORES)))
        outs = [np.asarray(r["out"]) for r in r1.results]
    else:
        nc, names, _, _ = _get_program((0, 1))
        res = run_bass_kernel_spmd(nc, maps(names, xs), core_ids=list(range(NCORES)))
        outs = [np.asarray(r["out"]) for r in res.results]

    full = np.concatenate([o.reshape(2, 2048, 1024) for o in outs], axis=0)
    return np.ascontiguousarray(full.astype(np.float32))


if __name__ == "__main__":
    import json, time
    t0 = time.time()
    nc, names, n_inst, peak = build_program((0, 1))
    print(f"build ok: {n_inst} instructions, arena peak {peak} words "
          f"({peak*4/1024:.0f} KB/partition), {time.time()-t0:.1f}s")


# version 09
# speedup vs baseline: 1.1598x; 1.1598x over previous
import numpy as np
import ml_dtypes

try:
    import concourse.bass as bass
except ImportError:
    import sys
    sys.path.insert(0, "/opt/trn_rl_repo")
    import concourse.bass as bass
import concourse.mybir as mybir
from concourse.bass_utils import run_bass_kernel_spmd

F32 = mybir.dt.float32
BF16 = mybir.dt.bfloat16
AF = mybir.ActivationFunctionType
ALU = mybir.AluOpType
AX = mybir.AxisListType
f32 = np.float32

NCORES = 8
SEQ = 2048
NSEQ = 2
DM = 1024
TPC = SEQ * NSEQ
B_L0 = 12.0
B_L1 = 3.0

SPLIT_LAYERS = False
PHASES = {"attn", "wout", "ffn"}


def _pos_enc():
    pos = np.arange(SEQ, dtype=np.float64)[:, None]
    div = np.exp(np.arange(0, 64, 2, dtype=np.float64) * -(np.log(10000.0) / 64))
    pe = np.zeros((SEQ, 64))
    pe[:, 0::2] = np.sin(pos * div)
    pe[:, 1::2] = np.cos(pos * div)
    pe_t = pe.T.astype(np.float32)
    return np.ascontiguousarray(np.concatenate([pe_t, pe_t], axis=0))


def _rope_cs():
    t = np.arange(1, SEQ + 1, dtype=np.float64)[:, None]
    i = np.arange(32, dtype=np.float64)
    theta = (1.0 / 10000.0) ** (2.0 * i / 64.0)
    ang = t * theta
    cos = np.cos(ang).T.astype(np.float32)
    sin = np.sin(ang).T.astype(np.float32)
    return np.ascontiguousarray(np.concatenate([cos, cos, sin, sin], axis=0))


def _masks():
    k = np.arange(128)[:, None]
    q = np.arange(512)[None, :]
    m = np.stack([(k <= q - 128 * o) for o in range(4)], 0).astype(np.float32)
    return np.ascontiguousarray(m.reshape(4 * 128, 512))


def _prep_shared(ins):
    C = np.ascontiguousarray
    d = {}
    wqkv = ins["L0_attn_W_qkv"]
    blocks = []
    for g in range(8):
        blocks.append(np.concatenate(
            [wqkv[:, g * 128:(g + 1) * 128],
             wqkv[:, 1024 + g * 128:1024 + (g + 1) * 128],
             wqkv[:, 2048 + g * 128:2048 + (g + 1) * 128]], axis=1))
    d["l0_wqkv"] = C(np.stack(blocks, 0))
    bq = ins["L0_attn_b_qkv"]
    d["l0_bqkv"] = C(np.stack(
        [np.concatenate([bq[g * 128:(g + 1) * 128],
                         bq[1024 + g * 128:1024 + (g + 1) * 128],
                         bq[2048 + g * 128:2048 + (g + 1) * 128]])
         for g in range(8)]).reshape(24, 128).T)
    d["l0_wout"] = C(ins["L0_attn_W_out"])
    d["l0_bout"] = C(ins["L0_attn_b_out"][None, :])
    d["l0_rw"] = C(ins["L0_ffn_router_weight"].T)
    d["l0_ebias"] = C(ins["L0_ffn_e_score_bias"][None, :])
    gu = ins["L0_ffn_gate_up_proj"].transpose(0, 2, 1)
    dn = ins["L0_ffn_down_proj"].transpose(0, 2, 1)
    sg = ins["L0_ffn_shared_gate_w"].T
    su = ins["L0_ffn_shared_up_w"].T
    sd = ins["L0_ffn_shared_down_w"].T
    sh_gu = [np.concatenate([sg[:, j * 512:(j + 1) * 512],
                             su[:, j * 512:(j + 1) * 512]], axis=1) for j in range(2)]
    d["gu_all"] = C(np.concatenate([gu, np.stack(sh_gu, 0)], 0))
    sh_dn = [sd[j * 512:(j + 1) * 512, :] for j in range(2)]
    d["dn_all"] = C(np.concatenate([dn, np.stack(sh_dn, 0)], 0))
    wq, wk, wv = ins["L1_attn_Wq"].T, ins["L1_attn_Wk"].T, ins["L1_attn_Wv"].T
    bqv, bkv, bvv = ins["L1_attn_bq"], ins["L1_attn_bk"], ins["L1_attn_bv"]
    blocks, brows = [], []
    for g in range(8):
        ha, hb = 2 * g, 2 * g + 1
        perm = ([ha * 64 + 2 * j for j in range(32)] +
                [hb * 64 + 2 * j for j in range(32)] +
                [ha * 64 + 2 * j + 1 for j in range(32)] +
                [hb * 64 + 2 * j + 1 for j in range(32)])
        vcols = list(range(ha * 64, ha * 64 + 128))
        blocks.append(np.concatenate([wq[:, perm], wk[:, perm], wv[:, vcols]], 1))
        brows.append(np.concatenate([bqv[perm], bkv[perm], bvv[vcols]]))
    d["l1_wqkv"] = C(np.stack(blocks, 0))
    d["l1_bqkv"] = C(np.concatenate(brows).reshape(24, 128).T)
    d["l1_wo"] = C(ins["L1_attn_Wo"].T)
    d["l1_bo"] = C(ins["L1_attn_bo"][None, :])
    d["l1_wT"] = C(ins["L1_ffn_W"].T)
    d["l1_b"] = C(ins["L1_ffn_b"][None, :])
    d["l1_ldT"] = C(ins["L1_ffn_lora_down"].T)
    d["l1_luT"] = C(ins["L1_ffn_lora_up"].T)
    d["l1_da"] = C(ins["L1_ffn_down_aux"])
    d["l1_uaT"] = C(ins["L1_ffn_up_aux"].T)
    d["c_penc"] = _pos_enc()
    d["c_ropecs"] = _rope_cs()
    d["c_masks"] = _masks()
    d["c_ident"] = np.eye(128, dtype=np.float32)
    return d


ENGS = ("sync", "scalar", "vector", "tensor", "gpsimd")


class Em:
    def __init__(self):
        self.streams = {k: [] for k in ENGS}
        self.cnt = {}
        self.lastw = {k: {} for k in ENGS}
        self.sems = {}

    def wait(self, eng, tag):
        if tag is None:
            return
        s, v = tag
        if v <= 0 or self.lastw[eng].get(s, 0) >= v:
            return
        self.lastw[eng][s] = v
        self.streams[eng].append(("wait_ge", {"sem": self.sems[s], "val": v}, None, 0))

    def waits(self, eng, tags):
        for t in tags:
            self.wait(eng, t)

    def dma(self, eng, sem, dst, src):
        self.wait(eng, (sem, self.cnt.get(sem, 0)))
        return self.emit(eng, "dma_start", {"out": dst, "in_": src}, sem=sem, inc=16)

    OWN = {"vector": "ve", "scalar": "se", "gpsimd": "po"}

    def emit(self, eng, meth, kwargs, sem=None, inc=1, chain=None):
        h = None
        v = 0
        own = chain if chain is not None else self.OWN.get(eng)
        if own is not None and meth != "wait_ge":
            self.wait(eng, (own, self.cnt.get(own, 0)))
            if sem is None and meth != "dma_start":
                sem = own
        if sem is not None:
            self.cnt[sem] = self.cnt.get(sem, 0) + inc
            v = self.cnt[sem]
            h = self.sems[sem]
        self.streams[eng].append((meth, kwargs, h, inc))
        return (sem, v) if sem is not None else None

    DMA_SEMS = ("wg", "ws0", "ws1", "wm0", "wm1", "xl0", "xl1",
                "az0", "az1", "ar0", "ar1", "xa")

    def fence(self):
        marks = []
        for eng, sem in (("tensor", "pe"), ("vector", "ve"), ("scalar", "se"),
                         ("gpsimd", "po")):
            pass
        marks0 = [("s2", self.cnt.get("s2", 0)), ("s3", self.cnt.get("s3", 0))]
        for eng, sem in (("tensor", "pe"), ("vector", "ve"), ("scalar", "se"),
                         ("gpsimd", "po")):
            st = self.streams[eng]
            i = len(st) - 1
            while i >= 0 and st[i][0] in ("wait_ge", "nop"):
                i -= 1
            if i >= 0 and st[i][2] is None:
                self.cnt[sem] = self.cnt.get(sem, 0) + 1
                st[i] = (st[i][0], st[i][1], self.sems[sem], 1)
            marks.append((sem, self.cnt.get(sem, 0)))
        marks += marks0
        for ssem in self.DMA_SEMS:
            marks.append((ssem, self.cnt.get(ssem, 0)))
        for eng in ENGS:
            for t in marks:
                if t is not None and t[0] is not None:
                    self.wait(eng, t)

    def finalize(self, block):
        for name in ENGS:
            items = self.streams[name]
            if not items:
                continue

            def body(e, items=items):
                for meth, kwargs, semh, inc in items:
                    if meth == "wait_ge":
                        e.wait_ge(kwargs["sem"], kwargs["val"])
                        continue
                    inst = getattr(e, meth)(**kwargs)
                    if semh is not None:
                        inst.then_inc(semh, inc)
            getattr(block, name)(body)


class Arena:
    def __init__(self, handle, words):
        self.h = handle
        self.top = 0
        self.words = words
        self.stack = []
        self.peak = 0

    def push(self):
        self.stack.append(self.top)

    def pop(self):
        self.top = self.stack.pop()

    def f32(self, cols):
        base = self.top
        self.top += cols
        self.peak = max(self.peak, self.top)
        assert self.top <= self.words, f"SBUF arena overflow: {self.top}"
        return self.h[:, base:base + cols]

    def bf16(self, cols):
        w = (cols + 1) // 2
        base = self.top
        self.top += w
        self.peak = max(self.peak, self.top)
        assert self.top <= self.words, f"SBUF arena overflow: {self.top}"
        return self.h[:, base:base + w].bitcast(BF16)


def build_program(layers=(0, 1)):
    nc = bass.Bass("TRN2", target_bir_lowering=False)
    in_names = []

    x_in = nc.dram_tensor("x", [TPC, DM], F32, kind="ExternalInput").ap()
    in_names.append("x")
    out = nc.dram_tensor("out", [TPC, DM], F32, kind="ExternalOutput").ap()
    xmid = (nc.dram_tensor("xmid", [TPC, DM], F32, kind="Internal").ap()
            if len(layers) == 2 else None)
    azspill = nc.dram_tensor("azspill", [8, 128, SEQ], BF16, kind="Internal").ap()

    shapes = {
        "l0_wqkv": (8, 1024, 384), "l0_bqkv": (128, 24),
        "l0_wout": (1024, 1024), "l0_bout": (1, 1024),
        "l0_rw": (1024, 32), "l0_ebias": (1, 32),
        "gu_all": (34, 1024, 1024), "dn_all": (34, 512, 1024),
        "l1_wqkv": (8, 1024, 384), "l1_bqkv": (128, 24),
        "l1_wo": (1024, 1024), "l1_bo": (1, 1024),
        "l1_wT": (1024, 1024), "l1_b": (1, 1024),
        "l1_ldT": (100, 4), "l1_luT": (4, 50),
        "l1_da": (100, 1024), "l1_uaT": (50, 1024),
        "c_penc": (128, 2048), "c_ropecs": (128, 2048),
        "c_masks": (512, 512), "c_ident": (128, 128),
    }
    used = set(shapes)
    if 0 not in layers:
        used -= {"l0_wqkv", "l0_bqkv", "l0_wout", "l0_bout", "l0_rw", "l0_ebias",
                 "gu_all", "dn_all", "c_penc"}
    if 1 not in layers:
        used -= {"l1_wqkv", "l1_bqkv", "l1_wo", "l1_bo", "l1_wT", "l1_b",
                 "l1_ldT", "l1_luT", "l1_da", "l1_uaT", "c_ropecs", "c_masks"}
    D = {}
    for k in sorted(used):
        D[k] = nc.dram_tensor(k, list(shapes[k]), F32, kind="ExternalInput").ap()
        in_names.append(k)

    E = Em()
    ctxs = []

    def enter(cm):
        ctxs.append(cm)
        return cm.__enter__()

    arena_h = enter(nc.sbuf_tensor("arena", [128, 52000], F32))
    A = Arena(arena_h, 52000)
    PB = [enter(nc.psum_tensor(f"pb{i}", [128, 512], F32)) for i in range(6)]
    PBbf = [enter(nc.psum_tensor(f"pbb{i}", [128, 1024], BF16)) for i in range(2)]
    PB = PB + [PBbf[0].ap().bitcast(F32) if hasattr(PBbf[0], 'ap') else PBbf[0][:, :].bitcast(F32),
               PBbf[1][:, :].bitcast(F32)]
    for sname in ("wg", "ws0", "ws1", "wm0", "wm1", "xl0", "xl1",
                  "az0", "az1", "ar0", "ar1", "xa",
                  "pe", "ve", "se", "s2", "s3", "po", "sp"):
        E.sems[sname] = enter(nc.semaphore(sname))
        E.cnt[sname] = 0
    block = enter(nc.Block())

    def dma_w(dst, src, sem="wg"):
        return E.dma("sync", sem, dst, src)

    def dma_x(dst, src, sem="xa"):
        return E.dma("scalar", sem, dst, src)

    def cast(dst, src, wait_tags=()):
        E.waits("gpsimd", wait_tags)
        return E.emit("gpsimd", "tensor_copy", {"out": dst, "in_": src}, sem="po")

    g_ones_bf = A.bf16(512)
    g_ones_f = A.f32(128)
    g_ident_f = A.f32(128)
    g_ident_bf = A.bf16(128)
    t_id = dma_w(g_ident_f[:, :], D["c_ident"])
    E.emit("gpsimd", "memset", {"ap": g_ones_bf[0:1, :], "constant": 1.0}, sem="po")
    E.emit("gpsimd", "memset", {"ap": g_ones_f[0:1, :], "constant": 1.0}, sem="po")
    E.wait("gpsimd", t_id)
    cast(g_ident_bf[:, :], g_ident_f[:, :])
    E.fence()

    first_layer, last_layer = layers[0], layers[-1]

    for layer in layers:
        src_x = x_in if layer == first_layer else xmid
        dst_x = out if layer == last_layer else xmid
        Bshift = f32(-(B_L0 if layer == 0 else B_L1))

        A.push()
        bqkv_col = A.f32(24)
        bo_bf = A.bf16(1024)
        l_stage = A.f32(2048)
        tc_bqkv = dma_w(bqkv_col[:, :], D["l0_bqkv" if layer == 0 else "l1_bqkv"])
        tb = dma_w(l_stage[0:1, 0:1024], D["l0_bout" if layer == 0 else "l1_bo"])
        tc_bo = cast(bo_bf[0:1, :], l_stage[0:1, 0:1024], (tb,))
        if layer == 1:
            bffn_bf = A.bf16(1024)
            E.wait("sync", tc_bo)
            tb = dma_w(l_stage[0:1, 1024:2048], D["l1_b"])
            tc_bffn = cast(bffn_bf[0:1, :], l_stage[0:1, 1024:2048], (tb,))
        if layer == 0:
            penc_sb = A.f32(2048)
            t_penc = dma_w(penc_sb[:, :], D["c_penc"])
            rwT_f = A.f32(256)
            t_rw = dma_w(rwT_f[:, :].rearrange("p (c e) -> p c e", e=32),
                         D["l0_rw"].rearrange("(c p) e -> p c e", p=128))
            ebias_b = A.f32(32)
            eb_row = A.f32(32)
            t_eb = dma_w(eb_row[0:1, :], D["l0_ebias"])
            E.waits("tensor", (t_eb, ("po", E.cnt["po"])))
            tmm = E.emit("tensor", "matmul",
                         {"out": PB[5][:, 0:32], "lhsT": g_ones_f[0:1, 0:128],
                          "rhs": eb_row[0:1, 0:32], "start": True, "stop": True}, sem="pe")
            E.wait("vector", tmm)
            t_ebb = E.emit("vector", "tensor_copy",
                           {"out": ebias_b[:, :], "in_": PB[5][:, 0:32]}, sem="ve")
        else:
            ropecs_sb = A.f32(2048)
            t_rope = dma_w(ropecs_sb[:, :], D["c_ropecs"])
            masks_bf = A.bf16(2048)
            E.wait("sync", tc_bffn)
            tm = dma_w(l_stage[:, :].rearrange("p (o q) -> p o q", q=512),
                       D["c_masks"].rearrange("(o p) q -> p o q", p=128))
            t_mb = cast(masks_bf[:, :], l_stage[:, :], (tm,))
        E.fence()

        for s in range(NSEQ):
            A.push()
            xn_fm = A.bf16(8 * 2048)
            acc = A.f32(16 * 1024)
            if layer == 0:
                combine = A.f32(16 * 32)

            A.push()
            xstage = A.f32(2 * 2048)
            sq = A.f32(2048)
            st = A.f32(16)
            xn32 = A.f32(2 * 2048)
            if layer == 0:
                xnT = A.f32(1024)
                lgall = A.f32(512)
                r_el = A.f32(512)
                r_rw = A.f32(512)
                r_cor = A.f32(512)
                r_eq = A.f32(512)
                r_sel = A.f32(512)
                r_m = A.f32(64)
            xstage_cons = [None, None]
            xload = [None, None]
            ln_bank = {0: (), 1: ()}
            sq_cons = None
            tr_last = [None, None]
            router_ev = None

            def w2(apv, o=1024):
                return apv.rearrange("p (w o) -> p w o", o=o)

            def bc2(col_ap, o=1024):
                t_ = col_ap.rearrange("p (w o) -> p w o", o=1)
                return bass.AP(t_.tensor, t_.offset,
                               [list(t_.ap[0]), list(t_.ap[1]), [0, o]])

            for bt in range(8):
                r0 = s * SEQ + bt * 256
                par = bt % 2
                if bt == 0:
                    xload[0] = dma_x(xstage[:, 0:2048], src_x[r0:r0 + 256, :]
                                     .rearrange("(w p) o -> p w o", p=128), sem="xl0")
                if bt + 1 < 8:
                    np_ = (bt + 1) % 2
                    E.wait("scalar", xstage_cons[np_])
                    xload[np_] = dma_x(xstage[:, np_ * 2048:(np_ + 1) * 2048]
                                       .rearrange("p (w o) -> p w o", o=1024),
                                       src_x[r0 + 256:r0 + 512, :]
                                       .rearrange("(w p) o -> p w o", p=128),
                                       sem=f"xl{np_}")
                xs = xstage[:, par * 2048:(par + 1) * 2048]
                E.wait("vector", xload[par])
                E.emit("vector", "tensor_reduce",
                       {"out": w2(st[:, 0:2], 1), "in_": w2(xs), "axis": AX.X,
                        "op": ALU.add}, sem="ve")
                E.waits("scalar", (xload[par], sq_cons))
                tsq = E.emit("scalar", "activation",
                             {"out": sq[:, :], "in_": xs, "func": AF.Square}, sem="se")
                E.wait("vector", tsq)
                sq_cons = E.emit("vector", "tensor_reduce",
                                 {"out": w2(st[:, 2:4], 1), "in_": w2(sq[:, :]),
                                  "axis": AX.X, "op": ALU.add}, sem="ve")
                E.emit("vector", "tensor_scalar",
                       {"out": st[:, 4:6], "in0": st[:, 0:2], "scalar1": 1.0 / 1024.0,
                        "scalar2": None, "op0": ALU.mult}, sem="ve")
                E.emit("vector", "tensor_tensor",
                       {"out": st[:, 6:8], "in0": st[:, 4:6], "in1": st[:, 4:6],
                        "op": ALU.mult}, sem="ve")
                tvar = E.emit("vector", "scalar_tensor_tensor",
                              {"out": st[:, 8:10], "in0": st[:, 2:4], "scalar": 1.0 / 1024.0,
                               "in1": st[:, 6:8], "op0": ALU.mult, "op1": ALU.subtract},
                              sem="ve")
                E.wait("scalar", tvar)
                tsd = E.emit("scalar", "activation",
                             {"out": st[:, 10:12], "in_": st[:, 8:10], "func": AF.Sqrt,
                              "bias": f32(1e-5)}, sem="se")
                E.wait("vector", tsd)
                E.emit("vector", "reciprocal", {"out": st[:, 12:14], "in_": st[:, 10:12]},
                       sem="ve")
                xv = xn32[:, par * 2048:(par + 1) * 2048]
                E.wait("vector", tr_last[par])
                E.emit("vector", "tensor_tensor",
                       {"out": w2(xv), "in0": w2(xs), "in1": bc2(st[:, 4:6]),
                        "op": ALU.subtract}, sem="ve")
                txn = E.emit("vector", "tensor_tensor",
                             {"out": w2(xv), "in0": w2(xv), "in1": bc2(st[:, 12:14]),
                              "op": ALU.mult}, sem="ve")
                xstage_cons[par] = E.emit("vector", "tensor_copy",
                       {"out": acc[:, bt * 2048:(bt + 1) * 2048], "in_": xs}, sem="ve")
                for sub in range(2):
                    tt = bt * 2 + sub
                    for c in range(8 if "notr" not in PHASES else 0):
                        bk = c % 2
                        E.wait("tensor", txn)
                        E.waits("tensor", ln_bank[bk])
                        ttr = E.emit("tensor", "matmul",
                                     {"out": PB[bk][:, 0:128],
                                      "lhsT": xv[:, sub * 1024 + c * 128: sub * 1024 + (c + 1) * 128],
                                      "rhs": g_ident_f[:, :], "start": True, "stop": True},
                                     sem="pe")
                        E.wait("scalar", ttr)
                        tev1 = E.emit("scalar", "copy",
                                      {"out": xn_fm[:, c * 2048 + tt * 128: c * 2048 + (tt + 1) * 128],
                                       "in_": PB[bk][:, 0:128]}, sem="se")
                        if layer == 0:
                            E.waits("vector", (ttr, tev1))
                            tev2 = E.emit("vector", "tensor_copy",
                                          {"out": xnT[:, c * 128:(c + 1) * 128],
                                           "in_": PB[bk][:, 0:128]}, sem="ve")
                            ln_bank[bk] = (tev1, tev2)
                        else:
                            ln_bank[bk] = (tev1,)
                    if layer == 0 and "norouter" not in PHASES and "notr" not in PHASES:
                        E.waits("tensor", (t_rw, router_ev) + ln_bank[0] + ln_bank[1])
                        tag = None
                        for c in range(8):
                            tag = E.emit("tensor", "matmul",
                                         {"out": PB[5][:, 0:32],
                                          "lhsT": xnT[:, c * 128:(c + 1) * 128],
                                          "rhs": rwT_f[:, c * 32:(c + 1) * 32],
                                          "start": c == 0, "stop": c == 7},
                                         sem="pe" if c == 7 else None)
                        E.wait("vector", tag)
                        router_ev = E.emit("vector", "tensor_copy",
                                           {"out": lgall[:, tt * 32:(tt + 1) * 32],
                                            "in_": PB[5][:, 0:32]}, sem="ve")
                tr_last[par] = ("pe", E.cnt["pe"])

            if layer == 0 and "norouter" not in PHASES and "notr" not in PHASES:
                def w3(apv, o=32):
                    return apv.rearrange("p (w o) -> p w o", o=o)

                def bcast16(col_ap):
                    t_ = col_ap.rearrange("p (w o) -> p w o", o=1)
                    return bass.AP(t_.tensor, t_.offset,
                                   [list(t_.ap[0]), list(t_.ap[1]), [0, 32]])

                mrow = r_m[:, 0:16]
                E.emit("vector", "tensor_reduce",
                       {"out": w3(mrow, 1), "in_": w3(lgall), "axis": AX.X,
                        "op": ALU.max}, sem="ve")
                tsub = E.emit("vector", "tensor_tensor",
                              {"out": w3(r_el), "in0": w3(lgall), "in1": bcast16(mrow),
                               "op": ALU.subtract}, sem="ve")
                E.wait("scalar", tsub)
                tex = E.emit("scalar", "activation",
                             {"out": r_el[:, :], "in_": r_el[:, :], "func": AF.Exp},
                             sem="se")
                E.wait("vector", tex)
                smrow = r_m[:, 16:32]
                E.emit("vector", "tensor_reduce",
                       {"out": w3(smrow, 1), "in_": w3(r_el), "axis": AX.X,
                        "op": ALU.add}, sem="ve")
                E.emit("vector", "reciprocal", {"out": smrow, "in_": smrow}, sem="ve")
                E.emit("vector", "tensor_tensor",
                       {"out": w3(r_rw), "in0": w3(r_el), "in1": bcast16(smrow),
                        "op": ALU.mult}, sem="ve")
                E.wait("vector", t_ebb)
                eb3 = bass.AP(ebias_b.tensor, ebias_b.offset,
                              [list(ebias_b.ap[0]), [0, 16], [1, 32]])
                E.emit("vector", "tensor_tensor",
                       {"out": w3(r_cor), "in0": w3(r_rw), "in1": eb3, "op": ALU.add},
                       sem="ve")
                E.emit("vector", "memset", {"ap": r_sel[:, :], "constant": 0.0}, sem="ve")
                m2row = r_m[:, 32:48]
                for _ in range(6):
                    E.emit("vector", "tensor_reduce",
                           {"out": w3(m2row, 1), "in_": w3(r_cor), "axis": AX.X,
                            "op": ALU.max}, sem="ve")
                    E.emit("vector", "tensor_tensor",
                           {"out": w3(r_eq), "in0": w3(r_cor), "in1": bcast16(m2row),
                            "op": ALU.is_ge}, sem="ve")
                    E.emit("vector", "tensor_add",
                           {"out": r_sel[:, :], "in0": r_sel[:, :], "in1": r_eq[:, :]},
                           sem="ve")
                    E.emit("vector", "scalar_tensor_tensor",
                           {"out": r_cor[:, :], "in0": r_eq[:, :], "scalar": -1e9,
                            "in1": r_cor[:, :], "op0": ALU.mult, "op1": ALU.add}, sem="ve")
                E.emit("vector", "tensor_mul",
                       {"out": r_eq[:, :], "in0": r_rw[:, :], "in1": r_sel[:, :]}, sem="ve")
                tsrow = r_m[:, 48:64]
                E.emit("vector", "tensor_reduce",
                       {"out": w3(tsrow, 1), "in_": w3(r_eq), "axis": AX.X,
                        "op": ALU.add}, sem="ve")
                E.emit("vector", "tensor_scalar_max",
                       {"out": tsrow, "in0": tsrow, "scalar1": 1e-12}, sem="ve")
                E.emit("vector", "reciprocal", {"out": tsrow, "in_": tsrow}, sem="ve")
                E.emit("vector", "tensor_tensor",
                       {"out": w3(combine[:, :]), "in0": w3(r_eq), "in1": bcast16(tsrow),
                        "op": ALU.mult}, sem="ve")
            E.fence()
            A.pop()

            A.push()
            do_attn = "attn" in PHASES
            wstage = A.f32(2 * 768)
            wbf = A.bf16(2 * 3072)
            vg = A.bf16(2048)
            vT = A.bf16(16 * 130)
            probs = A.bf16(16 * 512)
            az = A.bf16(2 * 2048)
            zf = A.f32(512)
            rbrow = A.f32(2 * 512)
            rb_bf = A.bf16(2 * 512)
            if layer == 0:
                qg = A.bf16(2048)
                kg = A.bf16(2048)
                q_src, k_src = qg, kg
            else:
                qh = A.bf16(2048)
                kh = A.bf16(2048)
                sc2r = A.f32(1024)
                q_src, k_src = qh, kh

            wq_d = D["l0_wqkv" if layer == 0 else "l1_wqkv"]
            wload = [None, None]
            wcast = [None, None]
            w_cons = [None, None]
            probs_cons = [None] * 16
            az_cons = [None, None]
            qkv_cons = None
            vT_cons = None
            bank_ev = {0: None, 1: None, 5: None}
            zb_ev = [(), ()]
            bc_ev = None

            def load_group(g):
                par = g % 2
                E.wait("sync", w_cons[par])
                src3 = wq_d[g].rearrange("(c p) j -> p c j", p=128)
                tcc = None
                for hp in range(4):
                    E.wait("sync", tcc)
                    tld = dma_w(wstage[:, par * 768:(par + 1) * 768]
                                .rearrange("p (c j) -> p c j", j=384),
                                src3[:, 2 * hp:2 * hp + 2], sem=f"ws{par}")
                    tcc = cast(wbf[:, par * 3072 + hp * 768: par * 3072 + (hp + 1) * 768],
                               wstage[:, par * 768:(par + 1) * 768], (tld,))
                wload[par] = tcc
                wcast[par] = tcc

            if do_attn:
                load_group(0)
            for g in range(8 if do_attn else 0):
                par = g % 2
                if g + 1 < 8:
                    load_group(g + 1)
                wv = wbf[:, par * 3072:(par + 1) * 3072]
                tms = {}
                for p in range(3):
                    for tcn in range(4):
                        bk = (p * 4 + tcn) % 2
                        E.waits("tensor", (wcast[par], bank_ev[bk]))
                        for c in range(8):
                            tmm = E.emit("tensor", "matmul",
                                   {"out": PB[bk][:, :],
                                    "lhsT": wv[:, c * 384 + p * 128: c * 384 + (p + 1) * 128],
                                    "rhs": xn_fm[:, c * 2048 + tcn * 512: c * 2048 + (tcn + 1) * 512],
                                    "start": c == 0, "stop": c == 7},
                                   sem="pe" if c == 7 else None)
                        bcol = bqkv_col[:, g * 3 + p: g * 3 + p + 1]
                        if layer == 0:
                            if p == 0:
                                E.waits("vector", (tmm, qkv_cons, tc_bqkv))
                                tev = E.emit("vector", "tensor_scalar",
                                             {"out": qg[:, tcn * 512:(tcn + 1) * 512],
                                              "in0": PB[bk][:, :], "scalar1": bcol,
                                              "scalar2": None, "op0": ALU.add}, sem="ve")
                            elif p == 1:
                                E.waits("vector", (tmm, qkv_cons, t_penc, tc_bqkv))
                                tev = E.emit("vector", "scalar_tensor_tensor",
                                             {"out": kg[:, tcn * 512:(tcn + 1) * 512],
                                              "in0": PB[bk][:, :], "scalar": bcol,
                                              "in1": penc_sb[:, tcn * 512:(tcn + 1) * 512],
                                              "op0": ALU.add, "op1": ALU.add}, sem="ve")
                            else:
                                E.waits("vector", (tmm, vT_cons, tc_bqkv))
                                tev = E.emit("vector", "tensor_scalar",
                                             {"out": vg[:, tcn * 512:(tcn + 1) * 512],
                                              "in0": PB[bk][:, :], "scalar1": bcol,
                                              "scalar2": None, "op0": ALU.add}, sem="ve")
                            bank_ev[bk] = tev
                        else:
                            if p < 2:
                                dst = qh if p == 0 else kh
                                if tcn == 0:
                                    E.waits("gpsimd", (qkv_cons,))
                                    tms[p] = E.emit("gpsimd", "memset",
                                                    {"ap": dst[:, :], "constant": 0.0}, sem="po")
                                E.waits("vector", (tmm, tms[p], t_rope, qkv_cons, tc_bqkv))
                                E.emit("vector", "scalar_tensor_tensor",
                                       {"out": sc2r[0:64, 0:512], "in0": PB[bk][0:64, :],
                                        "scalar": bcol[0:64, :],
                                        "in1": ropecs_sb[0:64, tcn * 512:(tcn + 1) * 512],
                                        "op0": ALU.add, "op1": ALU.mult}, sem="ve")
                                E.emit("vector", "scalar_tensor_tensor",
                                       {"out": sc2r[0:64, 512:1024], "in0": PB[bk][64:128, :],
                                        "scalar": bcol[64:128, :],
                                        "in1": ropecs_sb[64:128, tcn * 512:(tcn + 1) * 512],
                                        "op0": ALU.add, "op1": ALU.mult}, sem="ve")
                                E.emit("vector", "tensor_sub",
                                       {"out": dst[0:32, tcn * 512:(tcn + 1) * 512],
                                        "in0": sc2r[0:32, 0:512], "in1": sc2r[0:32, 512:1024]},
                                       sem="ve")
                                tev = E.emit("vector", "tensor_sub",
                                             {"out": dst[64:96, tcn * 512:(tcn + 1) * 512],
                                              "in0": sc2r[32:64, 0:512],
                                              "in1": sc2r[32:64, 512:1024]}, sem="ve")
                            else:
                                E.waits("vector", (tmm, vT_cons, tc_bqkv))
                                tev = E.emit("vector", "tensor_scalar",
                                             {"out": vg[:, tcn * 512:(tcn + 1) * 512],
                                              "in0": PB[bk][:, :], "scalar1": bcol,
                                              "scalar2": None, "op0": ALU.add}, sem="ve")
                            bank_ev[bk] = tev

                E.waits("gpsimd", (vT_cons,))
                tvo = E.emit("gpsimd", "memset", {"ap": vT[:, :], "constant": 1.0}, sem="po")
                vt_bank = {0: None, 1: None}
                last_vt_ev = None
                for kt in range(16):
                    bb = kt % 2
                    E.waits("tensor", (bank_ev[0], bank_ev[1], vt_bank[bb]))
                    ttr = E.emit("tensor", "transpose",
                                 {"out": PBbf[bb][:, 0:128],
                                  "in_": vg[:, kt * 128:(kt + 1) * 128],
                                  "identity": g_ident_bf[:, :]}, sem="pe")
                    E.waits("vector", (ttr, tvo))
                    tev = E.emit("vector", "tensor_copy",
                                 {"out": vT[:, kt * 130:kt * 130 + 130].rearrange(
                                     "p (h d) -> p h d", h=2, d=65)[:, :, 0:64],
                                  "in_": PBbf[bb][:, 0:128].rearrange(
                                      "p (h d) -> p h d", h=2, d=64)}, sem="ve")
                    vt_bank[bb] = tev
                    last_vt_ev = tev

                def emit_epilogue(pend):
                    nonlocal bc_ev
                    h, qc, zb, zbank, av_last, trb, trc = pend
                    hsl = slice(h * 64, (h + 1) * 64)
                    E.waits("tensor", (trb, bc_ev))
                    tbc = E.emit("tensor", "matmul",
                                 {"out": PB[4][0:64, :], "lhsT": g_ones_bf[0:1, 0:64],
                                  "rhs": rb_bf[0:1, zbank * 512:zbank * 512 + 512],
                                  "start": True, "stop": True,
                                  "skip_group_check": True}, sem="pe")
                    bc_done[zbank] = tbc
                    E.waits("scalar", (av_last, bc_ev, trc))
                    tze = E.emit("scalar", "copy",
                                 {"out": zf[0:64, :], "in_": zb[0:64, :]}, sem="se")
                    zb_ev[zbank] = (tze, trc)
                    E.waits("vector", (tze, tbc, az_cons[par]))
                    bc_ev = E.emit("vector", "tensor_tensor",
                                   {"out": az[hsl, par * 2048 + qc * 512:
                                              par * 2048 + (qc + 1) * 512],
                                    "in0": zf[0:64, :], "in1": PB[4][0:64, :],
                                    "op": ALU.mult}, sem="ve")

                tiles = [(h, qc) for h in range(2) for qc in range(4)]
                readyq = {}
                slot_of = {}
                slot_rr = [0]

                def emit_scores(ti):
                    h, qc = tiles[ti]
                    hsl = slice(h * 64, (h + 1) * 64)
                    ktmax = 16 if layer == 0 else 4 * qc + 4
                    readyq[ti] = [None] * ktmax
                    for kt in range(ktmax):
                        sbk = [0, 1, 5][kt % 3]
                        slot = slot_rr[0]
                        slot_rr[0] = (slot + 1) % 16
                        slot_of[(ti, kt)] = slot
                        E.waits("tensor", (bank_ev[sbk], last_vt_ev))
                        tsc = E.emit("tensor", "matmul",
                                     {"out": PB[sbk][:, :],
                                      "lhsT": k_src[hsl, kt * 128:(kt + 1) * 128],
                                      "rhs": q_src[hsl, qc * 512:(qc + 1) * 512],
                                      "start": True, "stop": True,
                                      "skip_group_check": True}, sem="pe")
                        E.waits("scalar", (tsc, probs_cons[slot]))
                        texp = E.emit("scalar", "activation",
                                      {"out": probs[:, slot * 512:(slot + 1) * 512],
                                       "in_": PB[sbk][:, :], "func": AF.Exp,
                                       "bias": Bshift, "scale": f32(0.125)},
                                      chain=["se", "s2", "s3"][kt % 3])
                        bank_ev[sbk] = texp
                        readyq[ti][kt] = texp
                        if layer == 1 and kt >= 4 * qc:
                            o = kt - 4 * qc
                            E.waits("vector", (texp, t_mb, probs_cons[slot]))
                            readyq[ti][kt] = E.emit("vector", "tensor_tensor",
                                           {"out": probs[:, slot * 512:(slot + 1) * 512],
                                            "in0": probs[:, slot * 512:(slot + 1) * 512],
                                            "in1": masks_bf[:, o * 512:(o + 1) * 512],
                                            "op": ALU.mult}, sem="ve")

                def emit_av(ti, kt, zb, ktmax):
                    h, qc = tiles[ti]
                    slot = slot_of[(ti, kt)]
                    E.wait("tensor", readyq[ti][kt])
                    tav = E.emit("tensor", "matmul",
                                 {"out": zb[0:65, :],
                                  "lhsT": vT[:, kt * 130 + h * 65: kt * 130 + (h + 1) * 65],
                                  "rhs": probs[:, slot * 512:(slot + 1) * 512],
                                  "start": kt == 0, "stop": kt == ktmax - 1,
                                  "skip_group_check": True}, sem="pe")
                    probs_cons[slot] = tav
                    return tav

                zbank = 0
                pends = []
                bc_done = [None, None]
                emit_scores(0)
                for ti in range(len(tiles)):
                    h, qc = tiles[ti]
                    ktmax = 16 if layer == 0 else 4 * qc + 4
                    zb = PB[2 + zbank]
                    if len(pends) >= 2:
                        emit_epilogue(pends.pop(0))
                    E.waits("tensor", zb_ev[zbank])
                    av_last = None
                    if ti + 1 < len(tiles):
                        h2, qc2 = tiles[ti + 1]
                        ktmax2 = 16 if layer == 0 else 4 * qc2 + 4
                        hsl2 = slice(h2 * 64, (h2 + 1) * 64)
                        readyq[ti + 1] = [None] * ktmax2
                        for kt in range(max(ktmax, ktmax2)):
                            if kt < ktmax:
                                av_last = emit_av(ti, kt, zb, ktmax)
                            if kt < ktmax2:
                                sbk = [0, 1, 5][kt % 3]
                                slot = slot_rr[0]
                                slot_rr[0] = (slot + 1) % 16
                                slot_of[(ti + 1, kt)] = slot
                                E.waits("tensor", (bank_ev[sbk],))
                                tsc = E.emit("tensor", "matmul",
                                             {"out": PB[sbk][:, :],
                                              "lhsT": k_src[hsl2, kt * 128:(kt + 1) * 128],
                                              "rhs": q_src[hsl2, qc2 * 512:(qc2 + 1) * 512],
                                              "start": True, "stop": True,
                                              "skip_group_check": True}, sem="pe")
                                E.waits("scalar", (tsc, probs_cons[slot]))
                                texp = E.emit("scalar", "activation",
                                              {"out": probs[:, slot * 512:(slot + 1) * 512],
                                               "in_": PB[sbk][:, :], "func": AF.Exp,
                                               "bias": Bshift, "scale": f32(0.125)},
                                              chain=["se", "s2", "s3"][kt % 3])
                                bank_ev[sbk] = texp
                                readyq[ti + 1][kt] = texp
                                if layer == 1 and kt >= 4 * qc2:
                                    o = kt - 4 * qc2
                                    E.waits("vector", (texp, t_mb, probs_cons[slot]))
                                    readyq[ti + 1][kt] = E.emit("vector", "tensor_tensor",
                                                   {"out": probs[:, slot * 512:(slot + 1) * 512],
                                                    "in0": probs[:, slot * 512:(slot + 1) * 512],
                                                    "in1": masks_bf[:, o * 512:(o + 1) * 512],
                                                    "op": ALU.mult}, sem="ve")
                    else:
                        for kt in range(ktmax):
                            av_last = emit_av(ti, kt, zb, ktmax)
                    vT_cons = av_last
                    qkv_cons = ("pe", E.cnt["pe"])
                    E.waits("vector", (av_last, bc_done[zbank]))
                    trc = E.emit("vector", "reciprocal",
                                 {"out": rbrow[0:1, zbank * 512:zbank * 512 + 512],
                                  "in_": zb[64:65, :]}, sem="ve")
                    trb = E.emit("vector", "tensor_copy",
                                 {"out": rb_bf[0:1, zbank * 512:zbank * 512 + 512],
                                  "in_": rbrow[0:1, zbank * 512:zbank * 512 + 512]}, sem="ve")
                    pends.append((h, qc, zb, zbank, av_last, trb, trc))
                    zbank ^= 1
                for p_ in pends:
                    emit_epilogue(p_)
                pends = []
                E.wait("scalar", bc_ev)
                az_cons[par] = dma_x(azspill[g], az[:, par * 2048:(par + 1) * 2048],
                                     sem=f"az{par}")
                w_cons[par] = ("pe", E.cnt["pe"])
            E.fence()
            A.pop()

            A.push()
            do_wout = ("wout" in PHASES) and ("attn" in PHASES)
            wo_stage = A.f32(4096)
            wo_bf = A.bf16(8192)
            azr = A.bf16(2 * 1024)
            wo_d = D["l0_wout" if layer == 0 else "l1_wo"]
            if not do_wout:
                tl1 = tcw1 = tl2 = tcw2 = None
            tl1 = dma_w(wo_stage[:, :].rearrange("p (c o) -> p c o", o=1024),
                        wo_d[0:512].rearrange("(c p) o -> p c o", p=128))
            tcw1 = cast(wo_bf[:, 0:4096], wo_stage[:, :], (tl1,))
            E.wait("sync", tcw1)
            tl2 = dma_w(wo_stage[:, :].rearrange("p (c o) -> p c o", o=1024),
                        wo_d[512:1024].rearrange("(c p) o -> p c o", p=128))
            tcw2 = cast(wo_bf[:, 4096:8192], wo_stage[:, :], (tl2,))
            azld = [None, None]
            az_fr = [None, None]
            wo_bank = {0: None, 1: None}
            for tt in range(16 if do_wout else 0):
                par = tt % 2
                E.wait("scalar", az_fr[par])
                azld[par] = dma_x(
                    azr[:, par * 1024:(par + 1) * 1024].rearrange("p (c j) -> p c j", j=128),
                    azspill[:, :, tt * 128:(tt + 1) * 128].transpose([1, 0, 2]),
                    sem=f"ar{par}")
                for oc in range(2):
                    bk = oc
                    E.waits("tensor", (azld[par], tcw1, tcw2, wo_bank[bk], tc_bo))
                    for c in range(8):
                        E.emit("tensor", "matmul",
                               {"out": PB[bk][:, :],
                                "lhsT": azr[:, par * 1024 + c * 128: par * 1024 + (c + 1) * 128],
                                "rhs": wo_bf[:, c * 1024 + oc * 512: c * 1024 + (oc + 1) * 512],
                                "start": c == 0, "stop": False}, sem=None)
                    tmm = E.emit("tensor", "matmul",
                                 {"out": PB[bk][:, :], "lhsT": g_ones_bf[0:1, 0:128],
                                  "rhs": bo_bf[0:1, oc * 512:(oc + 1) * 512],
                                  "start": False, "stop": True}, sem="pe")
                    E.wait("vector", tmm)
                    tev = E.emit("vector", "tensor_add",
                                 {"out": acc[:, tt * 1024 + oc * 512: tt * 1024 + (oc + 1) * 512],
                                  "in0": acc[:, tt * 1024 + oc * 512: tt * 1024 + (oc + 1) * 512],
                                  "in1": PB[bk][:, :]}, sem="ve")
                    wo_bank[bk] = tev
                az_fr[par] = ("pe", E.cnt["pe"])
            E.fence()
            A.pop()

            A.push()
            if "ffn" not in PHASES:
                pass
            elif layer == 0:
                gstage = A.f32(2 * 2048)
                gu_bf = A.bf16(2 * 8192)
                dn_bf = A.bf16(2 * 4096)
                hmid = A.bf16(2048)
                silu_sb = A.bf16(512)
                gcast = [None, None]
                dcast = [None, None]
                g_cons = [None, None]
                gu_bank = {0: None, 1: None, 2: None, 3: None}
                dn_bank = {4: None, 5: None, 6: None, 7: None}
                hm_cons = None
                silu_cons = None

                def load_expert(e_idx):
                    par = e_idx % 2
                    E.wait("sync", g_cons[par])
                    for j in range(4):
                        sp = j % 2
                        t = dma_w(gstage[:, sp * 2048:(sp + 1) * 2048]
                                  .rearrange("p (c o) -> p c o", o=1024),
                                  D["gu_all"][e_idx, 2 * j * 128:(2 * j + 2) * 128, :]
                                  .rearrange("(c p) o -> p c o", p=128), sem=f"wm{sp}")
                        tcc = cast(gu_bf[:, par * 8192 + 2048 * j: par * 8192 + 2048 * (j + 1)],
                                   gstage[:, sp * 2048:(sp + 1) * 2048], (t,))
                        E.wait("sync", tcc)
                        if j == 3:
                            gcast[par] = tcc
                    for j in range(2):
                        sp = j % 2
                        t = dma_w(gstage[:, sp * 2048:(sp + 1) * 2048]
                                  .rearrange("p (c o) -> p c o", o=1024),
                                  D["dn_all"][e_idx, 2 * j * 128:(2 * j + 2) * 128, :]
                                  .rearrange("(c p) o -> p c o", p=128), sem=f"wm{sp}")
                        tcc = cast(dn_bf[:, par * 4096 + 2048 * j: par * 4096 + 2048 * (j + 1)],
                                   gstage[:, sp * 2048:(sp + 1) * 2048], (t,))
                        E.wait("sync", tcc)
                        if j == 1:
                            dcast[par] = tcc

                load_expert(0)
                for e_idx in range(34):
                    par = e_idx % 2
                    if e_idx + 1 < 34:
                        load_expert(e_idx + 1)
                    guv = gu_bf[:, par * 8192:(par + 1) * 8192]
                    dnv = dn_bf[:, par * 4096:(par + 1) * 4096]
                    for tb in range(4):
                        hm_ev = None
                        for j in range(4):
                            bg = (j % 2) * 2
                            bu = bg + 1
                            E.waits("tensor", (gcast[par], gu_bank[bg]))
                            for c in range(8):
                                E.emit("tensor", "matmul",
                                       {"out": PB[bg][:, :],
                                        "lhsT": guv[:, c * 1024 + j * 128: c * 1024 + (j + 1) * 128],
                                        "rhs": xn_fm[:, c * 2048 + tb * 512: c * 2048 + (tb + 1) * 512],
                                        "start": c == 0, "stop": c == 7},
                                       sem="pe" if c == 7 else None)
                            tg = ("pe", E.cnt["pe"])
                            E.waits("tensor", (gu_bank[bu],))
                            for c in range(8):
                                E.emit("tensor", "matmul",
                                       {"out": PB[bu][:, :],
                                        "lhsT": guv[:, c * 1024 + (j + 4) * 128: c * 1024 + (j + 5) * 128],
                                        "rhs": xn_fm[:, c * 2048 + tb * 512: c * 2048 + (tb + 1) * 512],
                                        "start": c == 0, "stop": c == 7},
                                       sem="pe" if c == 7 else None)
                            tu = ("pe", E.cnt["pe"])
                            E.waits("scalar", (tg, silu_cons))
                            tsi = E.emit("scalar", "activation",
                                         {"out": silu_sb[:, :], "in_": PB[bg][:, :],
                                          "func": AF.Silu}, sem="se")
                            gu_bank[bg] = tsi
                            E.waits("vector", (tsi, tu, hm_cons))
                            thm = E.emit("vector", "tensor_tensor",
                                         {"out": hmid[:, j * 512:(j + 1) * 512],
                                          "in0": silu_sb[:, :], "in1": PB[bu][:, :],
                                          "op": ALU.mult}, sem="ve")
                            gu_bank[bu] = thm
                            hm_ev = thm
                            silu_cons = thm
                        for tt2 in range(4):
                            tt_g = tb * 4 + tt2
                            for oc in range(2):
                                bk = 4 + ((tt2 * 2 + oc) % 4)
                                E.waits("tensor", (hm_ev, dcast[par], dn_bank[bk]))
                                for ic in range(4):
                                    E.emit("tensor", "matmul",
                                           {"out": PB[bk][:, :],
                                            "lhsT": hmid[:, ic * 512 + tt2 * 128:
                                                         ic * 512 + (tt2 + 1) * 128],
                                            "rhs": dnv[:, ic * 1024 + oc * 512:
                                                       ic * 1024 + (oc + 1) * 512],
                                            "start": ic == 0, "stop": ic == 3},
                                           sem="pe" if ic == 3 else None)
                                tdn = ("pe", E.cnt["pe"])
                                E.wait("vector", tdn)
                                asl = acc[:, tt_g * 1024 + oc * 512: tt_g * 1024 + (oc + 1) * 512]
                                if e_idx < 32:
                                    tev = E.emit("vector", "scalar_tensor_tensor",
                                                 {"out": asl, "in0": PB[bk][:, :],
                                                  "scalar": combine[:, tt_g * 32 + e_idx:
                                                                    tt_g * 32 + e_idx + 1],
                                                  "in1": asl, "op0": ALU.mult, "op1": ALU.add},
                                                 sem="ve")
                                else:
                                    tev = E.emit("vector", "tensor_add",
                                                 {"out": asl, "in0": asl, "in1": PB[bk][:, :]},
                                                 sem="ve")
                                dn_bank[bk] = tev
                        hm_cons = ("pe", E.cnt["pe"])
                    g_cons[par] = ("pe", E.cnt["pe"])
            else:
                sm = A.f32(2048)
                ld_bf = A.bf16(8)
                lu_bf = A.bf16(64)
                da_bf = A.bf16(1024)
                ua_bf = A.bf16(1024)
                m1_bf = A.bf16(1024)
                m2_bf = A.bf16(1024)
                weff = A.bf16(8192)
                wstage2 = A.f32(2 * 1024)
                t1_ = dma_w(sm[0:100, 0:4], D["l1_ldT"])
                c1_ = cast(ld_bf[0:100, 0:4], sm[0:100, 0:4], (t1_,))
                E.wait("sync", c1_)
                t2_ = dma_w(sm[0:4, 4:54], D["l1_luT"])
                c2_ = cast(lu_bf[0:4, 0:50], sm[0:4, 4:54], (t2_,))
                E.wait("sync", c2_)
                t3_ = dma_w(sm[0:100, 0:1024], D["l1_da"])
                c3_ = cast(da_bf[0:100, :], sm[0:100, 0:1024], (t3_,))
                t4_ = dma_w(sm[0:50, 1024:2048], D["l1_uaT"])
                c4_ = cast(ua_bf[0:50, :], sm[0:50, 1024:2048], (t4_,))
                ce = None
                for j in range(2):
                    E.waits("tensor", (c1_, c3_, ce))
                    tmj = E.emit("tensor", "matmul",
                                 {"out": PB[4][0:4, :], "lhsT": ld_bf[0:100, 0:4],
                                  "rhs": da_bf[0:100, j * 512:(j + 1) * 512],
                                  "start": True, "stop": True}, sem="pe")
                    E.wait("scalar", tmj)
                    ce = E.emit("scalar", "copy",
                                {"out": m1_bf[0:4, j * 512:(j + 1) * 512],
                                 "in_": PB[4][0:4, :]}, sem="se")
                ce2 = None
                for j in range(2):
                    E.waits("tensor", (c2_, ce, ce2))
                    tmj = E.emit("tensor", "matmul",
                                 {"out": PB[5][0:50, :], "lhsT": lu_bf[0:4, 0:50],
                                  "rhs": m1_bf[0:4, j * 512:(j + 1) * 512],
                                  "start": True, "stop": True}, sem="pe")
                    E.wait("scalar", tmj)
                    ce2 = E.emit("scalar", "copy",
                                 {"out": m2_bf[0:50, j * 512:(j + 1) * 512],
                                  "in_": PB[5][0:50, :]}, sem="se")
                wld = [None, None]
                weff_bank = {4: None, 5: None}
                wstage_cons = [None, None]
                for ic in range(8):
                    par2 = ic % 2
                    E.wait("sync", wstage_cons[par2])
                    wld[par2] = dma_w(wstage2[:, par2 * 1024:(par2 + 1) * 1024],
                                      D["l1_wT"][ic * 128:(ic + 1) * 128, :], sem=f"ws{par2}")
                    for oh in range(2):
                        bk = 4 + oh
                        E.waits("tensor", (ce2, c4_, weff_bank[bk]))
                        tmj = E.emit("tensor", "matmul",
                                     {"out": PB[bk][:, :],
                                      "lhsT": m2_bf[0:50, ic * 128:(ic + 1) * 128],
                                      "rhs": ua_bf[0:50, oh * 512:(oh + 1) * 512],
                                      "start": True, "stop": True}, sem="pe")
                        E.waits("vector", (tmj, wld[par2]))
                        tev = E.emit("vector", "scalar_tensor_tensor",
                                     {"out": weff[:, ic * 1024 + oh * 512:
                                                  ic * 1024 + (oh + 1) * 512],
                                      "in0": PB[bk][:, :], "scalar": 1.0,
                                      "in1": wstage2[:, par2 * 1024 + oh * 512:
                                                     par2 * 1024 + (oh + 1) * 512],
                                      "op0": ALU.mult, "op1": ALU.add}, sem="ve")
                        weff_bank[bk] = tev
                    wstage_cons[par2] = ("ve", E.cnt["ve"])
                tweff = ("ve", E.cnt["ve"])
                ffn_bank = {0: None, 1: None}
                for tt in range(16):
                    for oc in range(2):
                        bk = oc
                        E.waits("tensor", (tweff, ffn_bank[bk], tc_bffn))
                        for c in range(8):
                            E.emit("tensor", "matmul",
                                   {"out": PB[bk][:, :],
                                    "lhsT": xn_fm[:, c * 2048 + tt * 128: c * 2048 + (tt + 1) * 128],
                                    "rhs": weff[:, c * 1024 + oc * 512: c * 1024 + (oc + 1) * 512],
                                    "start": c == 0, "stop": False}, sem=None)
                        tmm = E.emit("tensor", "matmul",
                                     {"out": PB[bk][:, :], "lhsT": g_ones_bf[0:1, 0:128],
                                      "rhs": bffn_bf[0:1, oc * 512:(oc + 1) * 512],
                                      "start": False, "stop": True}, sem="pe")
                        E.wait("vector", tmm)
                        tev = E.emit("vector", "tensor_add",
                                     {"out": acc[:, tt * 1024 + oc * 512:
                                                 tt * 1024 + (oc + 1) * 512],
                                      "in0": acc[:, tt * 1024 + oc * 512:
                                                 tt * 1024 + (oc + 1) * 512],
                                      "in1": PB[bk][:, :]}, sem="ve")
                        ffn_bank[bk] = tev
            E.fence()
            A.pop()

            for tt in range(16):
                r0 = s * SEQ + tt * 128
                dma_x(dst_x[r0:r0 + 128, :], acc[:, tt * 1024:(tt + 1) * 1024])
            E.fence()
            A.pop()
        A.pop()

    E.wait("sync", ("xa", E.cnt.get("xa", 0)))
    E.finalize(block)
    for cm in reversed(ctxs):
        cm.__exit__(None, None, None)
    n_inst = sum(len(v) for v in E.streams.values())
    return nc, in_names, n_inst, A.peak


_cache = {}


def _get_program(layers):
    key = tuple(layers)
    if key not in _cache:
        _cache[key] = build_program(layers)
    return _cache[key]


def kernel(**inputs):
    shared = _prep_shared(inputs)
    x = np.ascontiguousarray(np.asarray(inputs["L0_attn_x"], dtype=np.float32)
                             .reshape(16, 2048, 1024))
    xs = [np.ascontiguousarray(x[2 * c:2 * c + 2].reshape(TPC, DM)) for c in range(NCORES)]

    def maps(names, xlist):
        return [{n: (xlist[c] if n == "x" else shared[n]) for n in names}
                for c in range(NCORES)]

    if SPLIT_LAYERS:
        nc0, names0, _, _ = _get_program((0,))
        r0 = run_bass_kernel_spmd(nc0, maps(names0, xs), core_ids=list(range(NCORES)))
        x1s = [np.ascontiguousarray(np.asarray(r["out"])) for r in r0.results]
        nc1, names1, _, _ = _get_program((1,))
        r1 = run_bass_kernel_spmd(nc1, maps(names1, x1s), core_ids=list(range(NCORES)))
        outs = [np.asarray(r["out"]) for r in r1.results]
    else:
        nc, names, _, _ = _get_program((0, 1))
        res = run_bass_kernel_spmd(nc, maps(names, xs), core_ids=list(range(NCORES)))
        outs = [np.asarray(r["out"]) for r in res.results]

    full = np.concatenate([o.reshape(2, 2048, 1024) for o in outs], axis=0)
    return np.ascontiguousarray(full.astype(np.float32))


if __name__ == "__main__":
    import json, time
    t0 = time.time()
    nc, names, n_inst, peak = build_program((0, 1))
    print(f"build ok: {n_inst} instructions, arena peak {peak} words "
          f"({peak*4/1024:.0f} KB/partition), {time.time()-t0:.1f}s")
